# version 07
# baseline (speedup 1.0000x reference)
import numpy as np
import concourse.bass as bass
import concourse.mybir as mybir
from concourse.bass_utils import run_bass_kernel_spmd

F32 = mybir.dt.float32
BF16 = mybir.dt.bfloat16
I32 = mybir.dt.int32
AF = mybir.ActivationFunctionType
OP = mybir.AluOpType
AXX = mybir.AxisListType.X

S = 2048
D = 1024
NCH = 8
NH = 16
NE = 8
G = 8
NTS = 4
TS_ = 512

BUILD_STAGE = 5
SUBSTAGE = 9
PROJ_EVAC = "act"
PROJ_LIMIT = 999
DBG_DROP = ()
LN_DBG = "full"
FFN_NE = 8
DEBUG_OUTS = False

OFF_X = 0
OFF_XNQ = 65536
OFF_KH = 98304
OFF_VW = 131072
OFF_CT1 = 164352
OFF_CT2 = 180736
OFF_SPARE = 197120
OFF_SMALL = 205312
ARENA_BYTES = 212512



def _blocks_lhsT(w):
    I, O = w.shape
    ni, no = I // 128, O // 128
    b = w.reshape(ni, 128, no, 128).transpose(2, 0, 1, 3)
    return np.ascontiguousarray(b.transpose(2, 0, 1, 3).reshape(128, no * ni, 128))


def _perm_qk():
    perm = np.zeros(D, np.int64)
    for h in range(NH):
        for w in range(64):
            g2, r = divmod(w, 32)
            part, i16 = divmod(r, 16)
            perm[64 * h + w] = 64 * h + 2 * (g2 * 16 + i16) + part
    return perm


def _rope_tables():
    perm = _perm_qk()
    tab = np.zeros((64, S), np.float64)
    pos = np.arange(S, dtype=np.float64)
    for r in range(32):
        f_old = int(perm[r])
        p = f_old // 2
        part = f_old % 2
        theta = 10000.0 ** (-2.0 * p) / 512.0
        ang = pos * theta
        tab[r] = np.cos(ang)
        tab[32 + r] = np.sin(ang) * (1.0 if part == 1 else -1.0)
    return tab


C_ONE = 0
C_ONES16 = 1
C_IDN16 = 129
C_RSCALE = 145
C_EPS = 146
C_BDQ = 147
C_BDKV = 151
C_BUK = 155
C_BKR = 163
C_BUQ = 171
C_BQR = 179
C_BO = 187
C_B1 = 195
C_B2 = 451
NCOLS = 515

NROWS = 1024


def prep_host(inp):
    perm = _perm_qk()
    f32 = lambda a: np.asarray(a, np.float32)
    f64 = lambda a: np.asarray(a, np.float64)
    shared = {}

    cols = np.zeros((128, NCOLS), np.float64)
    cols[:, C_ONE] = 1.0
    for c in range(NCH):
        cols[0:64, C_ONES16 + c * 16 + 2 * c] = 1.0
        cols[64:128, C_ONES16 + c * 16 + 2 * c + 1] = 1.0
    for j in range(16):
        cols[j, C_IDN16 + j] = 1.0
    cols[:, C_RSCALE] = 1.0 / np.sqrt(np.float64(1040.0))
    cols[:, C_EPS] = 1e-5

    def put(cb, b):
        b = f64(b)
        cols[:, cb:cb + b.shape[0] // 128] = b.reshape(-1, 128).T
    put(C_BDQ, inp["L0_attn_Wd_q_b"])
    put(C_BDKV, inp["L0_attn_Wd_kv_b"])
    put(C_BUK, f64(inp["L0_attn_Wu_k_b"])[perm])
    put(C_BKR, f64(inp["L0_attn_Wk_r_b"])[perm])
    put(C_BUQ, f64(inp["L0_attn_Wu_q_b"])[perm])
    put(C_BQR, f64(inp["L0_attn_Wq_r_b"])[perm])
    put(C_BO, inp["L0_attn_Wo_b"])
    put(C_B1, f64(inp["L0_ffn_b1"]).reshape(-1))
    put(C_B2, f64(inp["L0_ffn_b2"]).reshape(-1))
    shared["cols"] = cols

    shared["rows"] = f64(inp["L0_attn_Wu_v_b"]).reshape(1, 1024)
    shared["ropetab"] = _rope_tables()

    shared["w_dq"] = _blocks_lhsT(f32(inp["L0_attn_Wd_q_w"]))
    shared["w_dkv"] = _blocks_lhsT(f32(inp["L0_attn_Wd_kv_w"]))
    shared["w_kr"] = _blocks_lhsT(f32(inp["L0_attn_Wk_r_w"])[:, perm])
    shared["w_uk"] = _blocks_lhsT(f32(inp["L0_attn_Wu_k_w"])[:, perm])
    shared["w_uq"] = _blocks_lhsT(f32(inp["L0_attn_Wu_q_w"])[:, perm])
    shared["w_qr"] = _blocks_lhsT(f32(inp["L0_attn_Wq_r_w"])[:, perm])
    shared["w_uv"] = np.ascontiguousarray(
        f32(inp["L0_attn_Wu_v_w"]).reshape(4, 128, D).transpose(1, 0, 2))
    shared["w_o"] = _blocks_lhsT(f32(inp["L0_attn_Wo_w"]))

    w1 = f32(inp["L0_ffn_W1"])
    w2 = f32(inp["L0_ffn_W2"])
    shared["w1b"] = np.ascontiguousarray(
        np.stack([_blocks_lhsT(w1[e]).reshape(128, 32, 8, 128).transpose(1, 0, 2, 3)
                  for e in range(NE)]))
    w2b = np.stack([_blocks_lhsT(w2[e]) for e in range(NE)])
    w2b = w2b.reshape(NE, 128, 8, 4, G, 128).transpose(0, 3, 1, 2, 4, 5)
    shared["w2g"] = np.ascontiguousarray(w2b.reshape(NE, 4, 128, 64, 128))
    return shared



class Ctr:
    def __init__(self, handle, name):
        self.h = handle
        self.name = name
        self.n = 0


class Emitter:
    ENGS = ("sync", "gpsimd", "vector", "scalar", "tensor")

    def __init__(self):
        self.plan = {e: [] for e in self.ENGS}
        self.waited = {e: {} for e in self.ENGS}
        self.bankg = {b: {} for b in range(16)}

    def wait(self, eng, ctr, val):
        if ctr is None or val is None or val <= 0:
            return
        if self.waited[eng].get(ctr.name, -1) >= val:
            return
        self.waited[eng][ctr.name] = val
        h = ctr.h
        self.plan[eng].append(lambda e, h=h, v=val: e.wait_ge(h, v))

    def op(self, eng, f, inc=None, k=1, waits=()):
        for cv in waits:
            if cv is not None:
                self.wait(eng, cv[0], cv[1])
        if inc is not None:
            inc.n += k
            h, kk = inc.h, k
            self.plan[eng].append(lambda e, f=f, h=h, kk=kk: f(e).then_inc(h, kk))
            return inc.n
        self.plan[eng].append(f)
        return None


def _banks(ten_id, c0, c1):
    b0 = 0 if ten_id == 0 else 8
    return range(b0 + c0 // TS_, b0 + (-(-c1 // TS_)))


def build_program(stage=5, dbg=False):
    nc = bass.Bass("TRN2", target_bir_lowering=False, enable_partition_id=False,
                   detect_race_conditions=False, dynamic_dma_scratch_size=16384)

    x_in = nc.dram_tensor("x_in", [D, S], F32, kind="ExternalInput").ap()
    colsD = nc.dram_tensor("cols", [128, NCOLS], F32, kind="ExternalInput").ap()
    rowsD = nc.dram_tensor("rows", [1, NROWS], F32, kind="ExternalInput").ap()
    ropeD = nc.dram_tensor("ropetab", [64, S], F32, kind="ExternalInput").ap()
    wD = {}
    for nm, shp in [("w_dq", [128, 32, 128]), ("w_dkv", [128, 32, 128]),
                    ("w_kr", [128, 64, 128]), ("w_uk", [128, 32, 128]),
                    ("w_uq", [128, 32, 128]), ("w_qr", [128, 32, 128]),
                    ("w_o", [128, 64, 128])]:
        wD[nm] = nc.dram_tensor(nm, shp, F32, kind="ExternalInput").ap()
    wuvD = nc.dram_tensor("w_uv", [128, 4, D], F32, kind="ExternalInput").ap()
    w1D = nc.dram_tensor("w1b", [NE, 32, 128, 8, 128], F32, kind="ExternalInput").ap()
    w2D = nc.dram_tensor("w2g", [NE, 4, 128, 64, 128], F32, kind="ExternalInput").ap()
    out_x = nc.dram_tensor("out_x", [D, S], F32, kind="ExternalOutput").ap()

    dbg_t = {}
    if dbg:
        for nm, shp, dt in [("d_xn", [128, NCH * S], BF16),
                            ("d_q", [128, NCH * S], BF16),
                            ("d_k", [128, NCH * S], BF16),
                            ("d_vt", [128, 16 * 1040], BF16),
                            ("d_x2", [D, S], F32),
                            ("d_mrow", [1, S], F32),
                            ("d_cb", [128, 16], F32),
                            ("d_oh0", [128, 8192], BF16),
                            ("d_oh1", [128, 8192], BF16),
                            ("d_h", [128, NCH * S], BF16),
                            ("d_mask", [128, 2048], BF16),
                            ("d_w2", [128, 8192], BF16)]:
            dbg_t[nm] = nc.dram_tensor(nm, shp, dt, kind="ExternalOutput").ap()

    nc.alloc_sbuf_tensor("arena", [128, ARENA_BYTES], mybir.dt.uint8)
    base = nc.lookup_mloc("arena").addr

    def sb(name, shape, dtype, off):
        return nc.alloc_sbuf_tensor_at(name, shape, dtype, offset=base + off)

    def _ap_sb(name, shape, dtype, off):
        return sb(name, shape, dtype, off)[tuple(slice(0, s) for s in shape)]

    xres = sb("xres", [128, NCH * S], F32, OFF_X)
    xnq = sb("xnq", [128, NCH * S], BF16, OFF_XNQ)
    khB = sb("khB", [128, NCH * S], BF16, OFF_KH)
    vwB = sb("vwB", [128, 16640], BF16, OFF_VW)
    ct1 = sb("ct1", [128, 8192], BF16, OFF_CT1)
    ct1w = sb("ct1w", [128, 4096], BF16, OFF_CT1)
    ct2 = sb("ct2", [128, 8192], BF16, OFF_CT2)
    murow = _ap_sb("murow", [1, S], F32, OFF_CT2 + 0)
    rstdrow = _ap_sb("rstdrow", [1, S], F32, OFF_CT2 + 8192)
    vrow = _ap_sb("vrow", [1, S], F32, OFF_CT1 + 0)
    trow = _ap_sb("trow", [1, S], F32, OFF_CT1 + 8192)
    hi1 = _ap_sb("hi1", [1, S], I32, OFF_CT2 + 0)
    sprow_sp = _ap_sb("sprow_sp", [1, S], F32, OFF_CT2 + 0)
    tmprow = _ap_sb("tmprow", [1, S], F32, OFF_CT2 + 8192)
    mrow = _ap_sb("mrow", [1, S], F32, OFF_CT2 + 8192)
    eqa = _ap_sb("eqa", [1, S], F32, OFF_CT1 + 0)
    eqb = _ap_sb("eqb", [1, S], F32, OFF_CT1 + 8192)
    wrowB = _ap_sb("wrowB", [1, S], BF16, OFF_CT2 + 0)
    spF = sb("spF", [128, 2048], F32, OFF_SPARE)
    ropeT = sb("ropeT", [64, S], F32, OFF_SPARE)
    spB = sb("spB", [128, 4096], BF16, OFF_SPARE)
    sprow = _ap_sb("sprow", [1, 1024], F32, OFF_SPARE + 4096)
    rinvrow = _ap_sb("rinvrow", [1, 1024], F32, OFF_SMALL + 2592)
    ct1F = sb("ct1F", [128, 2048], F32, OFF_CT1)
    colsS = sb("colsS", [128, NCOLS], F32, OFF_SMALL)
    onesr = _ap_sb("onesr", [1, 128], F32, OFF_SMALL + 2080)
    vbrow = _ap_sb("vbrow", [1, 1024], F32, OFF_SMALL + 2592)
    halfB = _ap_sb("halfB", [1, 128], BF16, OFF_SMALL + 6688)
    smalls = sb("smalls", [128, 64], F32, OFF_SMALL + 6944)

    PT = (0, 1024)
    MB = 0

    pA = nc.alloc_psum_tensor("pA", [128, S], F32)
    pB = nc.alloc_psum_tensor("pB", [128, S], F32)
    PTEN = (pA, pB)

    from contextlib import ExitStack
    ctx = ExitStack()
    ld = Ctr(ctx.enter_context(nc.semaphore("ld")), "ld")
    wst = Ctr(ctx.enter_context(nc.semaphore("wst")), "wst")
    pe = Ctr(ctx.enter_context(nc.semaphore("pe")), "pe")
    ve = Ctr(ctx.enter_context(nc.semaphore("ve")), "ve")
    ac = Ctr(ctx.enter_context(nc.semaphore("ac")), "ac")
    st = Ctr(ctx.enter_context(nc.semaphore("st")), "st")

    em = Emitter()


    def guards(tid, c0, c1):
        out = []
        for b in _banks(tid, c0, c1):
            out.extend(em.bankg[b].values())
        return out

    def setg(tid, c0, c1, ctr, val):
        for b in _banks(tid, c0, c1):
            em.bankg[b][ctr.name] = (ctr, val)

    def dma(q, dst, src, waits=()):
        ctr = ld if q == "sync" else wst
        depth = 2 if q == "sync" else 1
        n = ctr.n // 16
        w = list(waits)
        if n >= depth:
            w.append((ctr, 16 * (n - depth + 1)))
        return em.op(q, lambda e, d=dst, s=src: e.dma_start(d, s), inc=ctr, k=16,
                     waits=w)

    def mm(out_ap, lhsT, rhs, start, stop, waits=(), inc=None):
        return em.op("tensor",
                     lambda e, o=out_ap, l=lhsT, r=rhs, a=start, b=stop:
                     nc.tensor.matmul(o, l, r, start=a, stop=b),
                     inc=inc, waits=waits)

    dma("sync", xres[:, :].rearrange("p (c t) -> p c t", c=NCH),
        x_in.rearrange("(c p) t -> p c t", p=128))
    ld_x = ld.n
    if "xloads" not in DBG_DROP:
        dma("sync", colsS[:, :], colsD)
        dma("sync", vbrow[:, :], rowsD)
    ld_misc = ld.n
    if "xloads" not in DBG_DROP:
        dma("sync", ropeT[:, :], ropeD)
    ld_tab = ld.n
    em.op("vector", lambda e: e.memset(onesr[:, :], 1.0))
    ones_v = em.op("vector", lambda e: e.memset(halfB[:, :], 0.5), inc=ve)

    VW_DQ, VW_DKV, VW_KR, VW_UK, VW_UQ, VW_QR = 0, 4096, 8192, 0, 4096, 8192

    def wload(c0, name, nblk, waits=()):
        return dma("gpsimd",
                   vwB[:, c0:c0 + nblk * 128].rearrange("p (a f) -> p a f", f=128),
                   wD[name], waits=waits)

    w_dq_ld = wload(VW_DQ, "w_dq", 32)
    if "wloads" in DBG_DROP:
        w_dkv_ld = w_kr_ld = w_dq_ld
    else:
        w_dkv_ld = wload(VW_DKV, "w_dkv", 32)
        w_kr_ld = wload(VW_KR, "w_kr", 64)

    def layernorm(dst_bf, sq_c0s, wait_x_pe=None, wait_x_ve=None):
        sqt = sq_c0s
        pe_read = {}
        onec = colsS[:, C_ONE:C_ONE + 1]
        for c in range(NCH):
            sl = c % 2
            wts = [(ld, ld_x), (ld, ld_misc)]
            if wait_x_ve:
                wts.append((ve, wait_x_ve))
            if wait_x_pe:
                wts.append((pe, wait_x_pe))
            if c >= 2:
                wts.append((pe, pe_read[sl]))
            vv = em.op("vector", lambda e, c=c, sl=sl: e.tensor_mul(
                sqt[sl], xres[:, c * S:(c + 1) * S], xres[:, c * S:(c + 1) * S]),
                inc=ve, waits=wts)
            for ts in range(NTS):
                t0 = ts * TS_
                w0 = []
                if c == 0:
                    w0 = (guards(0, t0, t0 + TS_) + guards(1, t0, t0 + TS_)
                          + [(ld, ld_x), (ld, ld_misc)])
                    if wait_x_pe:
                        w0.append((pe, wait_x_pe))
                mm(pA[0:1, t0:t0 + TS_], onec, xres[:, c * S + t0:c * S + t0 + TS_],
                   start=(c == 0), stop=(c == NCH - 1), waits=w0)
                pv = mm(pB[0:1, t0:t0 + TS_], onec, sqt[sl][:, t0:t0 + TS_],
                        start=(c == 0), stop=(c == NCH - 1),
                        waits=[(ve, vv)], inc=(pe if ts == NTS - 1 else None))
            pe_read[sl] = pv
        pe_stats = pe.n
        em.op("vector", lambda e: e.tensor_scalar(murow, pA[0:1, 0:S], 1.0 / 1024.0,
                                                  None, OP.mult), waits=[(pe, pe_stats)])
        em.op("vector", lambda e: e.tensor_scalar(rstdrow, pB[0:1, 0:S],
                                                  1.0 / 1024.0, None, OP.mult))
        em.op("vector", lambda e: e.tensor_mul(pA[0:1, 0:S], murow, murow))
        vvar = em.op("vector", lambda e: e.tensor_sub(pB[0:1, 0:S], rstdrow,
                                                      pA[0:1, 0:S]), inc=ve)
        if LN_DBG == "noact":
            arow = em.op("scalar", lambda e: e.copy(rstdrow, pB[0:1, 0:S]),
                         inc=ac, waits=[(ve, vvar)])
        else:
            em.op("scalar", lambda e: e.activation(pA[0:1, 0:S], pB[0:1, 0:S], AF.Ln,
                                                   bias=colsS[0:1, C_EPS:C_EPS + 1]),
                  waits=[(ve, vvar)])
            arow = em.op("scalar", lambda e: e.activation(rstdrow, pA[0:1, 0:S],
                                                          AF.Exp, scale=-0.5), inc=ac)
        vdone = 0
        for ts in range(NTS):
            t0 = ts * TS_
            bw = (guards(0, t0, t0 + TS_) + guards(1, t0, t0 + TS_)
                  + [(ac, arow), (ve, ones_v)])
            mm(pA[:, t0:t0 + TS_], onesr[0:1, :], murow[0:1, t0:t0 + TS_], True, True,
               waits=bw)
            bb = mm(pB[:, t0:t0 + TS_], onesr[0:1, :], rstdrow[0:1, t0:t0 + TS_],
                    True, True, inc=pe)
            for c in range(NCH):
                em.op("vector", lambda e, c=c, t0=t0: e.tensor_sub(
                    sqt[0][:, 0:TS_], xres[:, c * S + t0:c * S + t0 + TS_],
                    pA[:, t0:t0 + TS_]), waits=[(pe, bb)])
                vdone = em.op("vector", lambda e, c=c, t0=t0: e.tensor_mul(
                    dst_bf[:, c * S + t0:c * S + t0 + TS_], sqt[0][:, 0:TS_],
                    pB[:, t0:t0 + TS_]), inc=ve)
            setg(0, t0, t0 + TS_, ve, vdone)
            setg(1, t0, t0 + TS_, ve, vdone)
        return vdone

    khF = sb("khF", [128, 4096], F32, OFF_KH)
    if LN_DBG in ("fake", "fakett"):
        _fv = 0
        for c in range(NCH):
            if LN_DBG == "fakett":
                _fv = em.op("vector", lambda e, c=c: e.scalar_tensor_tensor(
                    xnq[:, c * S:(c + 1) * S], xres[:, c * S:(c + 1) * S], 1.0,
                    xres[:, c * S:(c + 1) * S], OP.mult, OP.bypass),
                    inc=ve, waits=[(ld, ld_x), (ld, ld_misc)])
            else:
                _fv = em.op("vector", lambda e, c=c: e.tensor_copy(
                    xnq[:, c * S:(c + 1) * S], xres[:, c * S:(c + 1) * S]),
                    inc=ve, waits=[(ld, ld_x), (ld, ld_misc)])
        ln1_done = _fv
    else:
        ln1_done = layernorm(xnq, (khF[:, 0:S], khF[:, S:2 * S]))
    ln_pe_after = pe.n

    if dbg:
        em.op("sync", lambda e: e.dma_start(dbg_t["d_xn"], xnq[:, :]), inc=st, k=16,
              waits=[(ve, ln1_done)])

    def proj(dst_bf, src_bf, n_ic_src, wcol, n_oc, n_ic, bias_c, w_ld,
             rope=None, src2=None):
        last_ve = None
        last_ac = None
        last = 0
        ngrp = 0
        for oc in range(n_oc):
            for ts in range(NTS):
                if ngrp >= PROJ_LIMIT:
                    continue
                ngrp += 1
                t0 = ts * TS_
                sl = (ts % 2) * TS_
                fuse = rope is not None and oc > 0
                wts = guards(0, sl, sl + TS_) + [(wst, w_ld)]
                if rope is not None:
                    wts += [(wst, rope["w_ld"])]
                    if not fuse:
                        wts += guards(1, sl, sl + TS_)
                for ic in range(n_ic):
                    pv = mm(pA[:, sl:sl + TS_],
                            vwB[:, wcol + (oc * n_ic + ic) * 128:wcol + (oc * n_ic + ic) * 128 + 128],
                            src_bf[:, ic * S + t0:ic * S + t0 + TS_],
                            start=(ic == 0), stop=(ic == n_ic - 1 and not fuse),
                            waits=(wts if ic == 0 else []),
                            inc=(pe if (ic == n_ic - 1 and rope is None) else None))
                if rope is not None:
                    n2 = rope["n_ic"]
                    pt_r = pA if fuse else pB
                    for ic in range(n2):
                        pv = mm(pt_r[:, sl:sl + TS_],
                                vwB[:, rope["wcol"] + (oc * n2 + ic) * 128:rope["wcol"] + (oc * n2 + ic) * 128 + 128],
                                src2[:, ic * S + t0:ic * S + t0 + TS_],
                                start=False if fuse else (ic == 0),
                                stop=(ic == n2 - 1),
                                inc=(pe if ic == n2 - 1 else None))
                bcol = colsS[:, bias_c + oc:bias_c + oc + 1]
                dst = dst_bf[:, oc * S + t0:oc * S + t0 + TS_]
                if fuse:
                    vv = em.op("scalar", lambda e, dst=dst, sl=sl, oc=oc:
                               e.activation(dst, pA[:, sl:sl + TS_], AF.Identity,
                                            bias=smalls[:, rope["cbias"] + oc:
                                                        rope["cbias"] + oc + 1]),
                               inc=ac, waits=[(pe, pv), (ve, rope["cbias_v"])])
                    setg(0, sl, sl + TS_, ac, vv)
                    last_ac = vv
                elif rope is None:
                    if PROJ_EVAC == "skip":
                        last_ve = last_ve
                    elif PROJ_EVAC == "dve_imm":
                        vv = em.op("vector", lambda e, dst=dst, sl=sl:
                                   e.tensor_scalar(dst, pA[:, sl:sl + TS_], 0.5,
                                                   None, OP.add),
                                   inc=ve, waits=[(pe, pv)])
                        setg(0, sl, sl + TS_, ve, vv)
                        last_ve = vv
                    elif PROJ_EVAC == "none":
                        vv = em.op("vector", lambda e, dst=dst, sl=sl:
                                   e.tensor_copy(dst, pA[:, sl:sl + TS_]),
                                   inc=ve, waits=[(pe, pv)])
                        setg(0, sl, sl + TS_, ve, vv)
                        last_ve = vv
                    elif PROJ_EVAC == "act":
                        vv = em.op("scalar", lambda e, dst=dst, bcol=bcol, sl=sl:
                                   e.activation(dst, pA[:, sl:sl + TS_], AF.Identity,
                                                bias=bcol),
                                   inc=ac, waits=[(pe, pv)])
                        setg(0, sl, sl + TS_, ac, vv)
                        last_ac = vv
                    else:
                        vv = em.op("vector", lambda e, dst=dst, bcol=bcol, sl=sl:
                                   e.tensor_scalar(dst, pA[:, sl:sl + TS_], bcol,
                                                   None, OP.add),
                                   inc=ve, waits=[(pe, pv)])
                        setg(0, sl, sl + TS_, ve, vv)
                        last_ve = vv
                else:
                    rbcol = colsS[:, rope["bias_c"] + oc:rope["bias_c"] + oc + 1]
                    av = em.op("scalar", lambda e, dst=dst, bcol=bcol, sl=sl:
                               e.activation(dst, pA[:, sl:sl + TS_], AF.Identity,
                                            bias=bcol), inc=ac, waits=[(pe, pv)])
                    setg(0, sl, sl + TS_, ac, av)
                    em.op("vector", lambda e, sl=sl, rbcol=rbcol: e.tensor_scalar(
                        pB[:, sl:sl + TS_], pB[:, sl:sl + TS_], rbcol, None, OP.add),
                        waits=[(pe, pv)])
                    if oc == 0:
                        sc1, sc2 = 1024, 1536
                        shmask = list(range(16, 32)) + list(range(0, 16))
                        gw = guards(0, sc1, 2048)
                        em.op("vector", lambda e, sl=sl: e.stream_shuffle(
                            pA[0:32, sc1:sc1 + TS_], pB[0:32, sl:sl + TS_], shmask),
                            waits=[(ld, ld_tab)] + gw)
                        em.op("vector", lambda e, t0=t0: e.tensor_mul(
                            pA[0:32, sc1:sc1 + TS_], pA[0:32, sc1:sc1 + TS_],
                            ropeT[32:64, t0:t0 + TS_]))
                        em.op("vector", lambda e, sl=sl, t0=t0: e.tensor_mul(
                            pA[0:32, sc2:sc2 + TS_], pB[0:32, sl:sl + TS_],
                            ropeT[0:32, t0:t0 + TS_]), waits=[(ac, av)])
                        em.op("vector", lambda e, dst=dst: e.tensor_add(
                            dst[0:32, :], dst[0:32, :], pA[0:32, sc1:sc1 + TS_]))
                        em.op("vector", lambda e, dst=dst: e.tensor_add(
                            dst[0:32, :], dst[0:32, :], pA[0:32, sc2:sc2 + TS_]))
                        em.op("vector", lambda e, dst=dst, sl=sl: e.tensor_add(
                            dst[32:64, :], dst[32:64, :], pB[32:64, sl:sl + TS_]))
                        vv = em.op("vector", lambda e, dst=dst, sl=sl: e.tensor_add(
                            dst[64:128, :], dst[64:128, :], pB[64:128, sl:sl + TS_]),
                            inc=ve)
                        setg(0, sc1, 2048, ve, vv)
                    else:
                        vv = em.op("vector", lambda e, dst=dst, sl=sl: e.tensor_add(
                            dst, dst, pB[:, sl:sl + TS_]), inc=ve,
                            waits=[(ac, av)])
                    setg(0, sl, sl + TS_, ve, vv)
                    setg(1, sl, sl + TS_, ve, vv)
                    last_ve = vv
        out = []
        if last_ve is not None:
            out.append((ve, last_ve))
        if last_ac is not None:
            out.append((ac, last_ac))
        return out

    q_done = k_done = v_done = [(ve, ln1_done)]
    core_v = x2_done = ln1_done
    if stage >= 2:
        em.wait("tensor", ve, ln1_done)
        if SUBSTAGE >= 1:
            ctq_done = proj(ct1, xnq, NCH, VW_DQ, 4, 8, C_BDQ, w_dq_ld)
        if SUBSTAGE >= 2:
            ctkv_done = proj(ct2, xnq, NCH, VW_DKV, 4, 8, C_BDKV, w_dkv_ld)
        pe_ct = pe.n
        if "reload" in DBG_DROP:
            w_uk_ld = w_dq_ld
            w_uq_ld = w_dkv_ld
        else:
            w_uk_ld = wload(VW_UK, "w_uk", 32, waits=[(pe, pe_ct)])
            w_uq_ld = wload(VW_UQ, "w_uq", 32, waits=[(pe, pe_ct)])
        for oc in range(8 if "cbias" not in DBG_DROP else 0):
            em.op("vector", lambda e, oc=oc: e.tensor_add(
                smalls[:, 21 + oc:22 + oc], colsS[:, C_BUK + oc:C_BUK + oc + 1],
                colsS[:, C_BKR + oc:C_BKR + oc + 1]), waits=[(ld, ld_misc)])
        cb_v = em.op("vector", lambda e: e.engine_nop(), inc=ve) if False else None
        for oc in range(8 if "cbias" not in DBG_DROP else 0):
            kq_v = em.op("vector", lambda e, oc=oc: e.tensor_add(
                smalls[:, 29 + oc:30 + oc], colsS[:, C_BUQ + oc:C_BUQ + oc + 1],
                colsS[:, C_BQR + oc:C_BQR + oc + 1]),
                inc=(ve if oc == 7 else None))
        if "cbias" in DBG_DROP:
            kq_v = em.op("vector", lambda e: e.memset(smalls[:, 21:37], 0.125),
                         inc=ve)
        cbias_v = kq_v

        if SUBSTAGE < 3:
            k_done = q_done = v_done = [(ve, ln1_done)]
        if SUBSTAGE >= 3:
            k_done = proj(khB, ct2, 4, VW_UK, 8, 4, C_BUK, w_uk_ld,
                          rope={"wcol": VW_KR, "n_ic": 8, "bias_c": C_BKR,
                                "w_ld": w_kr_ld, "cbias": 21, "cbias_v": cbias_v},
                          src2=xnq)
        if SUBSTAGE >= 4:
            pe_k = pe.n
            w_qr_ld = wload(VW_QR, "w_qr", 32, waits=[(pe, pe_k)])
            q_done = proj(xnq, ct1, 4, VW_UQ, 8, 4, C_BUQ, w_uq_ld,
                          rope={"wcol": VW_QR, "n_ic": 4, "bias_c": C_BQR,
                                "w_ld": w_qr_ld, "cbias": 29, "cbias_v": cbias_v},
                          src2=ct1)

        if SUBSTAGE < 9:
            v_done = 0
            if dbg:
                em.op("sync", lambda e: e.dma_start(dbg_t["d_q"], xnq[:, :]),
                      inc=st, k=16, waits=list(q_done))
                em.op("sync", lambda e: e.dma_start(dbg_t["d_k"], khB[:, :]),
                      inc=st, k=16, waits=list(k_done))
        _skip_v = SUBSTAGE < 9
        if not _skip_v:
            pe_qk = pe.n
            w_uv_ld = dma("gpsimd",
                          ct1w[:, 0:4096].rearrange("p (c f) -> p c f", c=4),
                          wuvD, waits=list(q_done))
            for half in range(2):
                h0 = half * TS_
                bw = guards(0, h0, h0 + TS_) + [(ld, ld_misc)]
                bv = mm(pA[:, h0:h0 + TS_], onesr[0:1, :],
                        vbrow[0:1, h0:h0 + TS_], True, True,
                        waits=bw, inc=pe)
                vv = em.op("vector", lambda e, h0=h0: e.tensor_copy(
                    spF[:, h0:h0 + TS_], pA[:, h0:h0 + TS_]), inc=ve,
                    waits=[(pe, bv)] + list(q_done))
                setg(0, h0, h0 + TS_, ve, vv)
            vbb_v = vv
            vt_ones = em.op("vector", lambda e: e.memset(vwB[:, 64:16640:65], 1.0),
                            inc=ve, waits=list(k_done) + list(q_done) + [(pe, pe_qk)])
            for tt in range(16):
                t0 = tt * 128
                for dvh in range(2):
                    c0 = dvh * TS_
                    bw = guards(1, c0, c0 + TS_)
                    for cc in range(4):
                        pv = mm(pB[:, c0:c0 + TS_],
                                ct2[:, cc * S + t0:cc * S + t0 + 128],
                                ct1w[:, cc * 1024 + dvh * TS_:cc * 1024 + (dvh + 1) * TS_],
                                start=(cc == 0), stop=(cc == 3),
                                waits=(bw + [(wst, w_uv_ld), (ve, vt_ones), (ve, vbb_v)])
                                if cc == 0 else [], inc=(pe if cc == 3 else None))

                    def vevac(e, tt=tt, dvh=dvh, c0=c0):
                        op = vwB[:, tt * 1040 + dvh * 520: tt * 1040 + dvh * 520 + 520]
                        op = op.rearrange("p (h c) -> p h c", c=65)[:, :, 0:64]
                        return e.tensor_add(
                            op, pB[:, c0:c0 + TS_].rearrange("p (h c) -> p h c", c=64),
                            spF[:, c0:c0 + TS_].rearrange("p (h c) -> p h c", c=64))
                    vv = em.op("vector", vevac, inc=ve, waits=[(pe, pv)])
                    setg(1, c0, c0 + TS_, ve, vv)
            v_done = vv

        if dbg:
            em.op("sync", lambda e: e.dma_start(dbg_t["d_q"], xnq[:, :]), inc=st,
                  k=16, waits=list(q_done))
            em.op("sync", lambda e: e.dma_start(dbg_t["d_k"], khB[:, :]), inc=st,
                  k=16, waits=list(k_done))
            em.op("sync", lambda e: e.dma_start(dbg_t["d_vt"], vwB[:, 0:16640]),
                  inc=st, k=16, waits=[(ve, v_done)])

    if stage >= 3:
        def norms(src, out_col, wv):
            pv = None
            rdpe = {}
            for c in range(NCH):
                sl = c % 2
                wts = list(wv)
                if c >= 2:
                    wts.append((pe, rdpe[sl]))
                sqdst = spF[:, 0:S] if sl == 0 else ct1F[:, 0:S]
                sq = em.op("vector", lambda e, c=c, d=sqdst: e.tensor_mul(
                    d, src[:, c * S:(c + 1) * S],
                    src[:, c * S:(c + 1) * S]), inc=ve, waits=wts)
                for ts in range(NTS):
                    t0 = ts * TS_
                    w0 = guards(0, t0, t0 + TS_) if c == 0 else []
                    pv = mm(pA[0:16, t0:t0 + TS_],
                            colsS[:, C_ONES16 + c * 16:C_ONES16 + (c + 1) * 16],
                            sqdst[:, t0:t0 + TS_],
                            start=(c == 0), stop=(c == NCH - 1),
                            waits=w0 + [(ve, sq)],
                            inc=(pe if ts == NTS - 1 else None))
                rdpe[sl] = pv
            rv = em.op("vector", lambda e, oc=out_col: e.reduce_max(
                smalls[0:16, oc:oc + 1], pA[0:16, 0:S], axis=AXX), inc=ve,
                waits=[(pe, pv)])
            setg(0, 0, S, ve, rv)
            return rv

        nq = norms(xnq, 0, list(q_done) + [(ve, v_done)])
        nk = norms(khB, 1, list(k_done) + [(ve, v_done)])
        em.op("vector", lambda e: e.tensor_add(smalls[0:16, 2:3], smalls[0:16, 0:1],
                                               smalls[0:16, 1:2]),
              waits=[(ve, nq), (ve, nk)])
        cnv = em.op("vector", lambda e: e.tensor_scalar(
            smalls[0:16, 2:3], smalls[0:16, 2:3],
            colsS[0:16, C_RSCALE:C_RSCALE + 1], -0.5, OP.mult, OP.mult), inc=ve)
        tw = guards(1, 0, TS_)
        tpv = em.op("tensor", lambda e: nc.tensor.transpose(
            pB[0:1, 0:16], smalls[0:16, 2:3], colsS[0:16, C_IDN16:C_IDN16 + 16]),
            inc=pe, waits=tw + [(ve, cnv)])
        crv = em.op("vector", lambda e: e.tensor_copy(smalls[0:1, 3:19],
                                                      pB[0:1, 0:16]),
                    inc=ve, waits=[(pe, tpv)])
        setg(1, 0, TS_, ve, crv)
        cbv = mm(pB[:, 0:16], onesr[0:1, :], smalls[0:1, 3:19],
                 True, True, waits=guards(1, 0, TS_), inc=pe)
        cbev = em.op("vector", lambda e: e.tensor_copy(smalls[:, 4:20],
                                                       pB[:, 0:16]),
                     inc=ve, waits=[(pe, cbv)])
        setg(1, 0, TS_, ve, cbev)
        if dbg:
            em.op("sync", lambda e: e.dma_start(dbg_t["d_cb"], smalls[:, 4:20]),
                  inc=st, k=16, waits=[(ve, cbev)])

        ohalf = (ct1, ct2)
        pt_guard = {0: None, 1: None}
        rb_prev = None
        for qh in range(2):
            q0 = qh * 1024
            for h in range(NH):
                c = h // 2
                r0 = 64 * (h % 2)
                av_of = {}
                for kc in range(16):
                    stc = (kc % 2) * 1024
                    bw = guards(0, stc, stc + 1024)
                    for half in range(2):
                        pv = mm(pA[:, stc + half * TS_:stc + (half + 1) * TS_],
                                khB[r0:r0 + 64, c * S + kc * 128:c * S + kc * 128 + 128],
                                xnq[r0:r0 + 64, c * S + q0 + half * TS_:c * S + q0 + half * TS_ + TS_],
                                start=True, stop=True,
                                waits=(bw + [(ve, cbev)]) if half == 0 else [],
                                inc=(pe if half == 1 else None))
                    expw = [(pe, pv)]
                    if pt_guard[kc % 2] is not None:
                        expw.append(pt_guard[kc % 2])
                    ev = em.op("scalar", lambda e, stc=stc, h=h, kc=kc: e.activation(
                        spB[:, PT[kc % 2]:PT[kc % 2] + 1024], pA[:, stc:stc + 1024],
                        AF.Exp, bias=smalls[:, 4 + h:5 + h],
                        scale=colsS[:, C_RSCALE:C_RSCALE + 1]),
                        inc=ac, waits=expw)
                    setg(0, stc, stc + 1024, ac, ev)
                    ow = guards(1, 0, 1024) if kc == 0 else []
                    for half in range(2):
                        av = mm(pB[0:65, half * TS_:(half + 1) * TS_],
                                vwB[:, kc * 1040 + h * 65:kc * 1040 + h * 65 + 65],
                                spB[:, PT[kc % 2] + half * TS_:PT[kc % 2] + (half + 1) * TS_],
                                start=(kc == 0), stop=(kc == 15),
                                waits=(ow + [(ac, ev)]) if half == 0 else [],
                                inc=(pe if half == 1 else None))
                    av_of[kc] = av
                    pt_guard[kc % 2] = (pe, av)
                O = ohalf[qh]
                ov = em.op("vector", lambda e, O=O, c=c, r0=r0: e.tensor_copy(
                    O[r0:r0 + 64, c * 1024:(c + 1) * 1024], pB[0:64, 0:1024]),
                    inc=ve, waits=[(pe, av_of[15])])
                dv = em.op("scalar", lambda e: e.activation(
                    sprow[0:1, 0:1024], pB[64:65, 0:1024], AF.Ln), inc=ac,
                    waits=[(pe, av_of[15])])
                rvw = [] if rb_prev is None else [(pe, rb_prev)]
                rv = em.op("scalar", lambda e: e.activation(
                    rinvrow[0:1, 0:1024], sprow[0:1, 0:1024], AF.Exp, scale=-1.0),
                    inc=ac, waits=rvw)
                setg(1, 0, 1024, ve, ov)
                setg(1, 0, 1024, ac, dv)
                nw = guards(1, 1024, 2048)
                for half in range(2):
                    rb_prev = mm(pB[0:64, 1024 + half * TS_:1024 + (half + 1) * TS_],
                                 onesr[0:1, 0:64],
                                 rinvrow[0:1, half * TS_:(half + 1) * TS_],
                                 True, True,
                                 waits=(nw + [(ac, rv)]) if half == 0 else [], inc=pe)
                nv = em.op("vector", lambda e, O=O, c=c, r0=r0: e.tensor_mul(
                    O[r0:r0 + 64, c * 1024:(c + 1) * 1024],
                    O[r0:r0 + 64, c * 1024:(c + 1) * 1024], pB[0:64, 1024:2048]),
                    inc=ve, waits=[(pe, rb_prev), (ve, ov)])
                setg(1, 1024, 2048, ve, nv)
        core_v = ve.n
        if dbg:
            em.op("sync", lambda e: e.dma_start(dbg_t["d_oh0"], ct1[:, :]), inc=st,
                  k=16, waits=[(ve, core_v)])
            em.op("sync", lambda e: e.dma_start(dbg_t["d_oh1"], ct2[:, :]), inc=st,
                  k=16, waits=[(ve, core_v)])

        pe_core = pe.n
        wo_ld = dma("gpsimd",
                    khB[:, 0:8192].rearrange("p (a f) -> p a f", f=128),
                    wD["w_o"], waits=[(pe, pe_core)])
        for f in range(NCH):
            for qh in range(2):
                O = ohalf[qh]
                for half in range(2):
                    t0 = qh * 1024 + half * TS_
                    pc = (qh * 2 + half) * TS_
                    bw = guards(0, pc, pc + TS_)
                    for dm in range(NCH):
                        pv = mm(pA[:, pc:pc + TS_],
                                khB[:, (f * NCH + dm) * 128:(f * NCH + dm) * 128 + 128],
                                O[:, dm * 1024 + half * TS_:dm * 1024 + (half + 1) * TS_],
                                start=(dm == 0), stop=(dm == NCH - 1),
                                waits=(bw + [(wst, wo_ld), (ve, core_v)])
                                if dm == 0 else [],
                                inc=(pe if dm == NCH - 1 else None))
                    vv = em.op("vector", lambda e, f=f, t0=t0, pc=pc:
                               e.scalar_tensor_tensor(
                        xres[:, f * S + t0:f * S + t0 + TS_], pA[:, pc:pc + TS_],
                        colsS[:, C_BO + f:C_BO + f + 1],
                        xres[:, f * S + t0:f * S + t0 + TS_], OP.add, OP.add),
                        inc=ve, waits=[(pe, pv)])
                    setg(0, pc, pc + TS_, ve, vv)
        x2_done = ve.n
        if dbg:
            em.op("sync", lambda e: e.dma_start(
                dbg_t["d_x2"].rearrange("(c p) t -> p c t", p=128),
                xres[:, :].rearrange("p (c t) -> p c t", c=NCH)),
                inc=st, k=16, waits=[(ve, x2_done)])

    ln2_done = x2_done
    mrow_v = None
    if stage >= 4:
        pe_wo = pe.n
        ln2_done = layernorm(xnq, (khF[:, 0:S], khF[:, S:2 * S]),
                             wait_x_pe=pe_wo, wait_x_ve=x2_done)
        hv = None
        for ts in range(NTS):
            t0 = ts * TS_
            w0 = (guards(0, 0, S) + guards(1, 0, S) + [(ve, x2_done)]) if ts == 0 else []
            hv = mm(pA[0:1, t0:t0 + TS_], colsS[0:2, C_ONE:C_ONE + 1],
                    xres[0:2, t0:t0 + TS_], True, True, waits=w0,
                    inc=(pe if ts == NTS - 1 else None))
        em.op("vector", lambda e: e.tensor_scalar(vrow, murow, 2.0, None,
                                                  OP.mult),
              waits=[(pe, hv), (ve, ln2_done)])
        em.op("vector", lambda e: e.tensor_sub(pA[0:1, 0:S], pA[0:1, 0:S],
                                               vrow))
        em.op("vector", lambda e: e.tensor_mul(vrow, pA[0:1, 0:S], rstdrow))
        em.op("vector", lambda e: e.tensor_copy(hi1[0:1, :], vrow))
        em.op("vector", lambda e: e.tensor_copy(trow, hi1[0:1, :]))
        em.op("vector", lambda e: e.tensor_tensor(pA[0:1, 0:S], trow, vrow, OP.is_gt))
        em.op("vector", lambda e: e.tensor_tensor(pB[0:1, 0:S], trow, vrow, OP.is_lt))
        em.op("vector", lambda e: e.tensor_scalar(sprow_sp, vrow, 0.0, None, OP.is_ge))
        em.op("vector", lambda e: e.tensor_mul(pA[0:1, 0:S], pA[0:1, 0:S], sprow_sp))
        em.op("vector", lambda e: e.tensor_mul(tmprow, pB[0:1, 0:S], sprow_sp))
        em.op("vector", lambda e: e.tensor_sub(pB[0:1, 0:S], pB[0:1, 0:S], tmprow))
        em.op("vector", lambda e: e.tensor_sub(trow, trow, pA[0:1, 0:S]))
        em.op("vector", lambda e: e.tensor_add(trow, trow, pB[0:1, 0:S]))
        em.op("vector", lambda e: e.tensor_copy(hi1[0:1, :], trow))
        em.op("vector", lambda e: e.tensor_scalar(hi1[0:1, :], hi1[0:1, :], 64,
                                                  None, OP.add))
        em.op("vector", lambda e: e.tensor_scalar(hi1[0:1, :], hi1[0:1, :], 7,
                                                  None, OP.bitwise_and))
        mrow_v = em.op("vector", lambda e: e.tensor_copy(mrow, hi1[0:1, :]), inc=ve)
        setg(0, 0, S, ve, mrow_v)
        setg(1, 0, S, ve, mrow_v)

        if dbg:
            em.op("sync", lambda e: e.dma_start(dbg_t["d_mrow"], mrow), inc=st,
                  k=16, waits=[(ve, mrow_v)])

    if stage >= 5:
        W1O = (0, 1024)
        W2O = 2048

        w1_ld = {}
        w2_ld = {}
        w1_free = {}

        w1_ld[(0, 0)] = dma("gpsimd",
                            vwB[:, W1O[0]:W1O[0] + 1024].rearrange("p (b f) -> p b f", f=128),
                            w1D[0, 0],
                            waits=[(ve, x2_done)])
        w2_ld[(0, 0)] = dma("gpsimd",
                            vwB[:, W2O:W2O + 8192].rearrange("p (b f) -> p b f", f=128),
                            w2D[0, 0],
                            waits=[(ve, x2_done)])

        for e_ in range(min(NE, FFN_NE)):
            em.op("vector", lambda e, e_=e_: e.tensor_scalar(
                eqa, mrow, float(e_), None, OP.is_equal), waits=[(ve, mrow_v)])
            em.op("vector", lambda e, e_=e_: e.tensor_scalar(
                eqb, mrow, float((e_ - 1) % NE), None, OP.is_equal))
            wrv = em.op("vector", lambda e: e.tensor_add(wrowB[0:1, :], eqa, eqb),
                        inc=ve)
            mv = None
            for ts in range(NTS):
                t0 = ts * TS_
                bw = guards(1, t0, t0 + TS_)
                pv = mm(pB[:, t0:t0 + TS_], halfB[0:1, :],
                        wrowB[0:1, t0:t0 + TS_], True, True,
                        waits=bw + [(ve, wrv)], inc=pe)
                mv = em.op("vector", lambda e, t0=t0: e.tensor_copy(
                    spB[:, MB + t0:MB + t0 + TS_], pB[:, t0:t0 + TS_]), inc=ve,
                    waits=[(pe, pv)])
                setg(1, t0, t0 + TS_, ve, mv)
            mask_done = mv

            for g in range(4):
                h_last = None
                for ml in range(G):
                    m = g * G + ml
                    pt_id = ml % 2
                    pten = PTEN[pt_id]
                    slot = W1O[m % 2]
                    nm = m + 1
                    ne_, nm_ = (e_, nm) if nm < 32 else (e_ + 1, 0)
                    if ne_ < NE and (ne_, nm_) not in w1_ld:
                        w1_ld[(ne_, nm_)] = dma(
                            "gpsimd",
                            vwB[:, W1O[nm_ % 2]:W1O[nm_ % 2] + 1024].rearrange(
                                "p (b f) -> p b f", f=128),
                            w1D[ne_, nm_],
                            waits=[w1_free.get((ne_, nm_))])
                    pv = None
                    for ts in range(NTS):
                        t0 = ts * TS_
                        bw = guards(pt_id, t0, t0 + TS_)
                        for cc in range(NCH):
                            pv = mm(pten[:, t0:t0 + TS_],
                                    vwB[:, slot + cc * 128:slot + cc * 128 + 128],
                                    xnq[:, cc * S + t0:cc * S + t0 + TS_],
                                    start=(cc == 0), stop=(cc == NCH - 1),
                                    waits=(bw + [(wst, w1_ld[(e_, m)]),
                                                 (ve, ln2_done)]) if cc == 0 else [],
                                    inc=(pe if (cc == NCH - 1 and ts == NTS - 1)
                                         else None))
                    fm = m + 2
                    fe_, fm_ = (e_, fm) if fm < 32 else (e_ + 1, fm - 32)
                    w1_free[(fe_, fm_)] = (pe, pv)
                    for hts in range(NTS):
                        h0 = hts * TS_
                        hv = em.op("scalar", lambda e, m=m, pten=pten, e_=e_, h0=h0:
                                   e.activation(
                            khB[:, (m % G) * S + h0:(m % G) * S + h0 + TS_],
                            pten[:, h0:h0 + TS_], AF.Relu,
                            bias=colsS[:, C_B1 + e_ * 32 + m:
                                       C_B1 + e_ * 32 + m + 1]),
                            inc=ac, waits=([(pe, pv)] if hts == 0 else []))
                    setg(pt_id, 0, S, ac, hv)
                    h_last = em.op("vector", lambda e, m=m: e.tensor_mul(
                        khB[:, (m % G) * S:(m % G) * S + S],
                        khB[:, (m % G) * S:(m % G) * S + S],
                        spB[:, MB:MB + S]), inc=ve,
                        waits=[(ac, hv), (ve, mask_done)])
                yt = 0
                ylast_pe = None
                for f in range(NCH):
                    for ts in range(NTS):
                        t0 = ts * TS_
                        slot4 = yt % 4
                        yt += 1
                        pt_id = slot4 // 2
                        pten = PTEN[pt_id]
                        pc = (slot4 % 2) * TS_
                        bw = guards(pt_id, pc, pc + TS_)
                        for ml in range(G):
                            ylast_pe = mm(
                                pten[:, pc:pc + TS_],
                                vwB[:, W2O + (f * G + ml) * 128:W2O + (f * G + ml) * 128 + 128],
                                khB[:, (ml) * S + t0:(ml) * S + t0 + TS_],
                                start=(ml == 0),
                                stop=(ml == G - 1),
                                waits=(bw + [(wst, w2_ld[(e_, g)]),
                                             (ve, h_last)]) if ml == 0 else [],
                                inc=(pe if ml == G - 1 else None))
                        if g == 0:
                            em.op("vector", lambda e, f=f, t0=t0, e_=e_: e.tensor_scalar(
                                spF[:, 1024:1024 + TS_], spB[:, MB + t0:MB + t0 + TS_],
                                colsS[:, C_B2 + e_ * 8 + f:C_B2 + e_ * 8 + f + 1],
                                None, OP.mult), waits=[(pe, ylast_pe)])
                            em.op("vector", lambda e, f=f, t0=t0: e.tensor_add(
                                xres[:, f * S + t0:f * S + t0 + TS_],
                                spF[:, 1024:1024 + TS_],
                                xres[:, f * S + t0:f * S + t0 + TS_]))
                        yv = em.op("vector", lambda e, f=f, t0=t0, pten=pten, pc=pc:
                                   e.tensor_add(
                            xres[:, f * S + t0:f * S + t0 + TS_],
                            pten[:, pc:pc + TS_],
                            xres[:, f * S + t0:f * S + t0 + TS_]),
                            inc=ve, waits=[(pe, ylast_pe)])
                        setg(pt_id, pc, pc + TS_, ve, yv)
                if dbg and e_ == 0 and g == 0:
                    em.op("sync", lambda e: e.dma_start(dbg_t["d_h"], khB[:, :]),
                          inc=st, k=16, waits=[(ve, h_last)])
                    em.op("sync", lambda e: e.dma_start(dbg_t["d_mask"], spB[:, 0:S]),
                          inc=st, k=16, waits=[(ve, mask_done)])
                    em.op("sync", lambda e: e.dma_start(
                        dbg_t["d_w2"], vwB[:, W2O:W2O + 8192]),
                        inc=st, k=16, waits=[(wst, w2_ld[(0, 0)])])
                ng, ne2 = (g + 1, e_) if g < 3 else (0, e_ + 1)
                if ne2 < NE and (ne2, ng) not in w2_ld:
                    w2_ld[(ne2, ng)] = dma(
                        "gpsimd",
                        vwB[:, W2O:W2O + 8192].rearrange("p (b f) -> p b f", f=128),
                        w2D[ne2, ng],
                        waits=[(pe, ylast_pe)])

    final_v = ve.n
    out3 = out_x.rearrange("(c p) t -> c p t", p=128)
    for c in range(NCH):
        w = [(ve, final_v)] if c == 0 else []
        if c >= 2:
            w.append((st, st.n - 16))
        em.op("sync", lambda e, c=c, w=w: e.dma_start(
            out3[c], xres[:, c * S:(c + 1) * S]), inc=st, k=16, waits=w)
    em.wait("sync", ld, ld.n)
    em.wait("sync", wst, wst.n)
    em.op("sync", lambda e, v=st.n: e.wait_ge(st.h, v))

    with nc.Block() as block:
        for name, deco in (("sync", block.sync), ("gpsimd", block.gpsimd),
                           ("vector", block.vector), ("scalar", block.scalar),
                           ("tensor", block.tensor)):
            ops = em.plan[name]

            def body(eng, ops=ops):
                for f in ops:
                    f(eng)
            deco(body)
    ctx.close()
    return nc, dbg_t



def kernel(**inputs):
    shared = prep_host(inputs)
    x = np.asarray(inputs["L0_attn_x"], np.float32)
    nc, _ = build_program(stage=BUILD_STAGE, dbg=DEBUG_OUTS)
    in_maps = []
    for c in range(8):
        m = {"x_in": np.ascontiguousarray(x[c].T)}
        m.update(shared)
        in_maps.append(m)
    res = run_bass_kernel_spmd(nc, in_maps, core_ids=list(range(8)))
    out = np.stack([np.ascontiguousarray(res.results[c]["out_x"].T)
                    for c in range(8)])
    return out


# version 08 (final)
# speedup vs baseline: 1.0291x; 1.0291x over previous
import numpy as np
import concourse.bass as bass
import concourse.mybir as mybir
from concourse.bass_utils import run_bass_kernel_spmd

F32 = mybir.dt.float32
BF16 = mybir.dt.bfloat16
I32 = mybir.dt.int32
AF = mybir.ActivationFunctionType
OP = mybir.AluOpType
AXX = mybir.AxisListType.X

S = 2048
D = 1024
NCH = 8
NH = 16
NE = 8
G = 8
NTS = 4
TS_ = 512

BUILD_STAGE = 5
SUBSTAGE = 9
PROJ_EVAC = "act"
PROJ_LIMIT = 999
DBG_DROP = ()
LN_DBG = "full"
FFN_NE = 8
DEBUG_OUTS = False

OFF_X = 0
OFF_XNQ = 65536
OFF_KH = 98304
OFF_VW = 131072
OFF_CT1 = 164352
OFF_CT2 = 180736
OFF_SPARE = 197120
OFF_SMALL = 205312
ARENA_BYTES = 212512



def _blocks_lhsT(w):
    I, O = w.shape
    ni, no = I // 128, O // 128
    b = w.reshape(ni, 128, no, 128).transpose(2, 0, 1, 3)
    return np.ascontiguousarray(b.transpose(2, 0, 1, 3).reshape(128, no * ni, 128))


def _perm_qk():
    perm = np.zeros(D, np.int64)
    for h in range(NH):
        for w in range(64):
            g2, r = divmod(w, 32)
            part, i16 = divmod(r, 16)
            perm[64 * h + w] = 64 * h + 2 * (g2 * 16 + i16) + part
    return perm


def _rope_tables():
    perm = _perm_qk()
    tab = np.zeros((64, S), np.float64)
    pos = np.arange(S, dtype=np.float64)
    for r in range(32):
        f_old = int(perm[r])
        p = f_old // 2
        part = f_old % 2
        theta = 10000.0 ** (-2.0 * p) / 512.0
        ang = pos * theta
        tab[r] = np.cos(ang)
        tab[32 + r] = np.sin(ang) * (1.0 if part == 1 else -1.0)
    return tab


C_ONE = 0
C_ONES16 = 1
C_IDN16 = 129
C_RSCALE = 145
C_EPS = 146
C_BDQ = 147
C_BDKV = 151
C_BUK = 155
C_BKR = 163
C_BUQ = 171
C_BQR = 179
C_BO = 187
C_B1 = 195
C_B2 = 451
NCOLS = 515

NROWS = 1024


def prep_host(inp):
    perm = _perm_qk()
    f32 = lambda a: np.asarray(a, np.float32)
    f64 = lambda a: np.asarray(a, np.float64)
    shared = {}

    cols = np.zeros((128, NCOLS), np.float64)
    cols[:, C_ONE] = 1.0
    for c in range(NCH):
        cols[0:64, C_ONES16 + c * 16 + 2 * c] = 1.0
        cols[64:128, C_ONES16 + c * 16 + 2 * c + 1] = 1.0
    for j in range(16):
        cols[j, C_IDN16 + j] = 1.0
    cols[:, C_RSCALE] = 1.0 / np.sqrt(np.float64(1040.0))
    cols[:, C_EPS] = 1e-5

    def put(cb, b):
        b = f64(b)
        cols[:, cb:cb + b.shape[0] // 128] = b.reshape(-1, 128).T
    put(C_BDQ, inp["L0_attn_Wd_q_b"])
    put(C_BDKV, inp["L0_attn_Wd_kv_b"])
    put(C_BUK, f64(inp["L0_attn_Wu_k_b"])[perm])
    put(C_BKR, f64(inp["L0_attn_Wk_r_b"])[perm])
    put(C_BUQ, f64(inp["L0_attn_Wu_q_b"])[perm])
    put(C_BQR, f64(inp["L0_attn_Wq_r_b"])[perm])
    put(C_BO, inp["L0_attn_Wo_b"])
    put(C_B1, f64(inp["L0_ffn_b1"]).reshape(-1))
    put(C_B2, f64(inp["L0_ffn_b2"]).reshape(-1))
    shared["cols"] = cols

    shared["rows"] = f64(inp["L0_attn_Wu_v_b"]).reshape(1, 1024)
    shared["ropetab"] = _rope_tables()

    shared["w_dq"] = _blocks_lhsT(f32(inp["L0_attn_Wd_q_w"]))
    shared["w_dkv"] = _blocks_lhsT(f32(inp["L0_attn_Wd_kv_w"]))
    shared["w_kr"] = _blocks_lhsT(f32(inp["L0_attn_Wk_r_w"])[:, perm])
    shared["w_uk"] = _blocks_lhsT(f32(inp["L0_attn_Wu_k_w"])[:, perm])
    shared["w_uq"] = _blocks_lhsT(f32(inp["L0_attn_Wu_q_w"])[:, perm])
    shared["w_qr"] = _blocks_lhsT(f32(inp["L0_attn_Wq_r_w"])[:, perm])
    shared["w_uv"] = np.ascontiguousarray(
        f32(inp["L0_attn_Wu_v_w"]).reshape(4, 128, D).transpose(1, 0, 2))
    shared["w_o"] = _blocks_lhsT(f32(inp["L0_attn_Wo_w"]))

    w1 = f32(inp["L0_ffn_W1"])
    w2 = f32(inp["L0_ffn_W2"])
    shared["w1b"] = np.ascontiguousarray(
        np.stack([_blocks_lhsT(w1[e]).reshape(128, 32, 8, 128).transpose(1, 0, 2, 3)
                  for e in range(NE)]))
    w2b = np.stack([_blocks_lhsT(w2[e]) for e in range(NE)])
    w2b = w2b.reshape(NE, 128, 8, 4, G, 128).transpose(0, 3, 1, 2, 4, 5)
    shared["w2g"] = np.ascontiguousarray(w2b.reshape(NE, 4, 128, 64, 128))
    return shared



class Ctr:
    def __init__(self, handle, name):
        self.h = handle
        self.name = name
        self.n = 0


class Emitter:
    ENGS = ("sync", "gpsimd", "vector", "scalar", "tensor")

    def __init__(self):
        self.plan = {e: [] for e in self.ENGS}
        self.waited = {e: {} for e in self.ENGS}
        self.bankg = {b: {} for b in range(16)}

    def wait(self, eng, ctr, val):
        if ctr is None or val is None or val <= 0:
            return
        if self.waited[eng].get(ctr.name, -1) >= val:
            return
        self.waited[eng][ctr.name] = val
        h = ctr.h
        self.plan[eng].append(lambda e, h=h, v=val: e.wait_ge(h, v))

    def op(self, eng, f, inc=None, k=1, waits=()):
        for cv in waits:
            if cv is not None:
                self.wait(eng, cv[0], cv[1])
        if inc is not None:
            inc.n += k
            h, kk = inc.h, k
            self.plan[eng].append(lambda e, f=f, h=h, kk=kk: f(e).then_inc(h, kk))
            return inc.n
        self.plan[eng].append(f)
        return None


def _banks(ten_id, c0, c1):
    b0 = 0 if ten_id == 0 else 8
    return range(b0 + c0 // TS_, b0 + (-(-c1 // TS_)))


def build_program(stage=5, dbg=False):
    nc = bass.Bass("TRN2", target_bir_lowering=False, enable_partition_id=False,
                   detect_race_conditions=False, dynamic_dma_scratch_size=16384)

    x_in = nc.dram_tensor("x_in", [D, S], F32, kind="ExternalInput").ap()
    colsD = nc.dram_tensor("cols", [128, NCOLS], F32, kind="ExternalInput").ap()
    rowsD = nc.dram_tensor("rows", [1, NROWS], F32, kind="ExternalInput").ap()
    ropeD = nc.dram_tensor("ropetab", [64, S], F32, kind="ExternalInput").ap()
    wD = {}
    for nm, shp in [("w_dq", [128, 32, 128]), ("w_dkv", [128, 32, 128]),
                    ("w_kr", [128, 64, 128]), ("w_uk", [128, 32, 128]),
                    ("w_uq", [128, 32, 128]), ("w_qr", [128, 32, 128]),
                    ("w_o", [128, 64, 128])]:
        wD[nm] = nc.dram_tensor(nm, shp, F32, kind="ExternalInput").ap()
    wuvD = nc.dram_tensor("w_uv", [128, 4, D], F32, kind="ExternalInput").ap()
    w1D = nc.dram_tensor("w1b", [NE, 32, 128, 8, 128], F32, kind="ExternalInput").ap()
    w2D = nc.dram_tensor("w2g", [NE, 4, 128, 64, 128], F32, kind="ExternalInput").ap()
    out_x = nc.dram_tensor("out_x", [D, S], F32, kind="ExternalOutput").ap()

    dbg_t = {}
    if dbg:
        for nm, shp, dt in [("d_xn", [128, NCH * S], BF16),
                            ("d_q", [128, NCH * S], BF16),
                            ("d_k", [128, NCH * S], BF16),
                            ("d_vt", [128, 16 * 1040], BF16),
                            ("d_x2", [D, S], F32),
                            ("d_mrow", [1, S], F32),
                            ("d_cb", [128, 16], F32),
                            ("d_oh0", [128, 8192], BF16),
                            ("d_oh1", [128, 8192], BF16),
                            ("d_h", [128, NCH * S], BF16),
                            ("d_mask", [128, 2048], BF16),
                            ("d_w2", [128, 8192], BF16)]:
            dbg_t[nm] = nc.dram_tensor(nm, shp, dt, kind="ExternalOutput").ap()

    nc.alloc_sbuf_tensor("arena", [128, ARENA_BYTES], mybir.dt.uint8)
    base = nc.lookup_mloc("arena").addr

    def sb(name, shape, dtype, off):
        return nc.alloc_sbuf_tensor_at(name, shape, dtype, offset=base + off)

    def _ap_sb(name, shape, dtype, off):
        return sb(name, shape, dtype, off)[tuple(slice(0, s) for s in shape)]

    xres = sb("xres", [128, NCH * S], F32, OFF_X)
    xnq = sb("xnq", [128, NCH * S], BF16, OFF_XNQ)
    khB = sb("khB", [128, NCH * S], BF16, OFF_KH)
    vwB = sb("vwB", [128, 16640], BF16, OFF_VW)
    ct1 = sb("ct1", [128, 8192], BF16, OFF_CT1)
    ct1w = sb("ct1w", [128, 4096], BF16, OFF_CT1)
    ct2 = sb("ct2", [128, 8192], BF16, OFF_CT2)
    murow = _ap_sb("murow", [1, S], F32, OFF_CT2 + 0)
    rstdrow = _ap_sb("rstdrow", [1, S], F32, OFF_CT2 + 8192)
    vrow = _ap_sb("vrow", [1, S], F32, OFF_CT1 + 0)
    trow = _ap_sb("trow", [1, S], F32, OFF_CT1 + 8192)
    hi1 = _ap_sb("hi1", [1, S], I32, OFF_CT2 + 0)
    sprow_sp = _ap_sb("sprow_sp", [1, S], F32, OFF_CT2 + 0)
    tmprow = _ap_sb("tmprow", [1, S], F32, OFF_CT2 + 8192)
    mrow = _ap_sb("mrow", [1, S], F32, OFF_CT2 + 8192)
    eqa = _ap_sb("eqa", [1, S], F32, OFF_CT1 + 0)
    eqb = _ap_sb("eqb", [1, S], F32, OFF_CT1 + 8192)
    wrowB = _ap_sb("wrowB", [1, S], BF16, OFF_CT2 + 0)
    spF = sb("spF", [128, 2048], F32, OFF_SPARE)
    ropeT = sb("ropeT", [64, S], F32, OFF_SPARE)
    spB = sb("spB", [128, 4096], BF16, OFF_SPARE)
    sprow = _ap_sb("sprow", [1, 1024], F32, OFF_SPARE + 4096)
    rinvrow = _ap_sb("rinvrow", [1, 1024], F32, OFF_SMALL + 2592)
    ct1F = sb("ct1F", [128, 2048], F32, OFF_CT1)
    colsS = sb("colsS", [128, NCOLS], F32, OFF_SMALL)
    onesr = _ap_sb("onesr", [1, 128], F32, OFF_SMALL + 2080)
    vbrow = _ap_sb("vbrow", [1, 1024], F32, OFF_SMALL + 2592)
    halfB = _ap_sb("halfB", [1, 128], BF16, OFF_SMALL + 6688)
    smalls = sb("smalls", [128, 64], F32, OFF_SMALL + 6944)

    PT = (0, 1024)
    MB = 0

    pA = nc.alloc_psum_tensor("pA", [128, S], F32)
    pB = nc.alloc_psum_tensor("pB", [128, S], F32)
    PTEN = (pA, pB)

    from contextlib import ExitStack
    ctx = ExitStack()
    ld = Ctr(ctx.enter_context(nc.semaphore("ld")), "ld")
    wst = Ctr(ctx.enter_context(nc.semaphore("wst")), "wst")
    pe = Ctr(ctx.enter_context(nc.semaphore("pe")), "pe")
    ve = Ctr(ctx.enter_context(nc.semaphore("ve")), "ve")
    ac = Ctr(ctx.enter_context(nc.semaphore("ac")), "ac")
    st = Ctr(ctx.enter_context(nc.semaphore("st")), "st")

    em = Emitter()


    def guards(tid, c0, c1):
        out = []
        for b in _banks(tid, c0, c1):
            out.extend(em.bankg[b].values())
        return out

    def setg(tid, c0, c1, ctr, val):
        for b in _banks(tid, c0, c1):
            em.bankg[b][ctr.name] = (ctr, val)

    def dma(q, dst, src, waits=()):
        ctr = ld if q == "sync" else wst
        depth = 3
        n = ctr.n // 16
        w = list(waits)
        if n >= depth:
            w.append((ctr, 16 * (n - depth + 1)))
        return em.op(q, lambda e, d=dst, s=src: e.dma_start(d, s), inc=ctr, k=16,
                     waits=w)

    def mm(out_ap, lhsT, rhs, start, stop, waits=(), inc=None):
        return em.op("tensor",
                     lambda e, o=out_ap, l=lhsT, r=rhs, a=start, b=stop:
                     nc.tensor.matmul(o, l, r, start=a, stop=b),
                     inc=inc, waits=waits)

    dma("sync", xres[:, :].rearrange("p (c t) -> p c t", c=NCH),
        x_in.rearrange("(c p) t -> p c t", p=128))
    ld_x = ld.n
    if "xloads" not in DBG_DROP:
        dma("sync", colsS[:, :], colsD)
        dma("sync", vbrow[:, :], rowsD)
    ld_misc = ld.n
    if "xloads" not in DBG_DROP:
        dma("sync", ropeT[:, :], ropeD)
    ld_tab = ld.n
    em.op("vector", lambda e: e.memset(onesr[:, :], 1.0))
    ones_v = em.op("vector", lambda e: e.memset(halfB[:, :], 0.5), inc=ve)

    VW_DQ, VW_DKV, VW_KR, VW_UK, VW_UQ, VW_QR = 0, 4096, 8192, 0, 4096, 8192

    def wload(c0, name, nblk, waits=()):
        return dma("gpsimd",
                   vwB[:, c0:c0 + nblk * 128].rearrange("p (a f) -> p a f", f=128),
                   wD[name], waits=waits)

    w_dq_ld = wload(VW_DQ, "w_dq", 32)
    if "wloads" in DBG_DROP:
        w_dkv_ld = w_kr_ld = w_dq_ld
    else:
        w_dkv_ld = wload(VW_DKV, "w_dkv", 32)
        w_kr_ld = wload(VW_KR, "w_kr", 64)

    def layernorm(dst_bf, sq_c0s, wait_x_pe=None, wait_x_ve=None):
        sqt = sq_c0s
        pe_read = {}
        onec = colsS[:, C_ONE:C_ONE + 1]
        for c in range(NCH):
            sl = c % 2
            wts = [(ld, ld_x), (ld, ld_misc)]
            if wait_x_ve:
                wts.append((ve, wait_x_ve))
            if wait_x_pe:
                wts.append((pe, wait_x_pe))
            if c >= 2:
                wts.append((pe, pe_read[sl]))
            vv = em.op("vector", lambda e, c=c, sl=sl: e.tensor_mul(
                sqt[sl], xres[:, c * S:(c + 1) * S], xres[:, c * S:(c + 1) * S]),
                inc=ve, waits=wts)
            for ts in range(NTS):
                t0 = ts * TS_
                w0 = []
                if c == 0:
                    w0 = (guards(0, t0, t0 + TS_) + guards(1, t0, t0 + TS_)
                          + [(ld, ld_x), (ld, ld_misc)])
                    if wait_x_pe:
                        w0.append((pe, wait_x_pe))
                mm(pA[0:1, t0:t0 + TS_], onec, xres[:, c * S + t0:c * S + t0 + TS_],
                   start=(c == 0), stop=(c == NCH - 1), waits=w0)
                pv = mm(pB[0:1, t0:t0 + TS_], onec, sqt[sl][:, t0:t0 + TS_],
                        start=(c == 0), stop=(c == NCH - 1),
                        waits=[(ve, vv)], inc=(pe if ts == NTS - 1 else None))
            pe_read[sl] = pv
        pe_stats = pe.n
        em.op("vector", lambda e: e.tensor_scalar(murow, pA[0:1, 0:S], 1.0 / 1024.0,
                                                  None, OP.mult), waits=[(pe, pe_stats)])
        em.op("vector", lambda e: e.tensor_scalar(rstdrow, pB[0:1, 0:S],
                                                  1.0 / 1024.0, None, OP.mult))
        em.op("vector", lambda e: e.tensor_mul(pA[0:1, 0:S], murow, murow))
        vvar = em.op("vector", lambda e: e.tensor_sub(pB[0:1, 0:S], rstdrow,
                                                      pA[0:1, 0:S]), inc=ve)
        if LN_DBG == "noact":
            arow = em.op("scalar", lambda e: e.copy(rstdrow, pB[0:1, 0:S]),
                         inc=ac, waits=[(ve, vvar)])
        else:
            em.op("scalar", lambda e: e.activation(pA[0:1, 0:S], pB[0:1, 0:S], AF.Ln,
                                                   bias=colsS[0:1, C_EPS:C_EPS + 1]),
                  waits=[(ve, vvar)])
            arow = em.op("scalar", lambda e: e.activation(rstdrow, pA[0:1, 0:S],
                                                          AF.Exp, scale=-0.5), inc=ac)
        vdone = 0
        for ts in range(NTS):
            t0 = ts * TS_
            bw = (guards(0, t0, t0 + TS_) + guards(1, t0, t0 + TS_)
                  + [(ac, arow), (ve, ones_v)])
            mm(pA[:, t0:t0 + TS_], onesr[0:1, :], murow[0:1, t0:t0 + TS_], True, True,
               waits=bw)
            bb = mm(pB[:, t0:t0 + TS_], onesr[0:1, :], rstdrow[0:1, t0:t0 + TS_],
                    True, True, inc=pe)
            for c in range(NCH):
                em.op("vector", lambda e, c=c, t0=t0: e.tensor_sub(
                    sqt[0][:, 0:TS_], xres[:, c * S + t0:c * S + t0 + TS_],
                    pA[:, t0:t0 + TS_]), waits=[(pe, bb)])
                vdone = em.op("vector", lambda e, c=c, t0=t0: e.tensor_mul(
                    dst_bf[:, c * S + t0:c * S + t0 + TS_], sqt[0][:, 0:TS_],
                    pB[:, t0:t0 + TS_]), inc=ve)
            setg(0, t0, t0 + TS_, ve, vdone)
            setg(1, t0, t0 + TS_, ve, vdone)
        return vdone

    khF = sb("khF", [128, 4096], F32, OFF_KH)
    if LN_DBG in ("fake", "fakett"):
        _fv = 0
        for c in range(NCH):
            if LN_DBG == "fakett":
                _fv = em.op("vector", lambda e, c=c: e.scalar_tensor_tensor(
                    xnq[:, c * S:(c + 1) * S], xres[:, c * S:(c + 1) * S], 1.0,
                    xres[:, c * S:(c + 1) * S], OP.mult, OP.bypass),
                    inc=ve, waits=[(ld, ld_x), (ld, ld_misc)])
            else:
                _fv = em.op("vector", lambda e, c=c: e.tensor_copy(
                    xnq[:, c * S:(c + 1) * S], xres[:, c * S:(c + 1) * S]),
                    inc=ve, waits=[(ld, ld_x), (ld, ld_misc)])
        ln1_done = _fv
    else:
        ln1_done = layernorm(xnq, (khF[:, 0:S], khF[:, S:2 * S]))
    ln_pe_after = pe.n

    if dbg:
        em.op("sync", lambda e: e.dma_start(dbg_t["d_xn"], xnq[:, :]), inc=st, k=16,
              waits=[(ve, ln1_done)])

    def proj(dst_bf, src_bf, n_ic_src, wcol, n_oc, n_ic, bias_c, w_ld,
             rope=None, src2=None):
        last_ve = None
        last_ac = None
        last = 0
        ngrp = 0
        for oc in range(n_oc):
            for ts in range(NTS):
                if ngrp >= PROJ_LIMIT:
                    continue
                ngrp += 1
                t0 = ts * TS_
                sl = (ts % 2) * TS_
                fuse = rope is not None and oc > 0
                wts = guards(0, sl, sl + TS_) + [(wst, w_ld)]
                if rope is not None:
                    wts += [(wst, rope["w_ld"])]
                    if not fuse:
                        wts += guards(1, sl, sl + TS_)
                for ic in range(n_ic):
                    pv = mm(pA[:, sl:sl + TS_],
                            vwB[:, wcol + (oc * n_ic + ic) * 128:wcol + (oc * n_ic + ic) * 128 + 128],
                            src_bf[:, ic * S + t0:ic * S + t0 + TS_],
                            start=(ic == 0), stop=(ic == n_ic - 1 and not fuse),
                            waits=(wts if ic == 0 else []),
                            inc=(pe if (ic == n_ic - 1 and rope is None) else None))
                if rope is not None:
                    n2 = rope["n_ic"]
                    pt_r = pA if fuse else pB
                    for ic in range(n2):
                        pv = mm(pt_r[:, sl:sl + TS_],
                                vwB[:, rope["wcol"] + (oc * n2 + ic) * 128:rope["wcol"] + (oc * n2 + ic) * 128 + 128],
                                src2[:, ic * S + t0:ic * S + t0 + TS_],
                                start=False if fuse else (ic == 0),
                                stop=(ic == n2 - 1),
                                inc=(pe if ic == n2 - 1 else None))
                bcol = colsS[:, bias_c + oc:bias_c + oc + 1]
                dst = dst_bf[:, oc * S + t0:oc * S + t0 + TS_]
                if fuse:
                    vv = em.op("scalar", lambda e, dst=dst, sl=sl, oc=oc:
                               e.activation(dst, pA[:, sl:sl + TS_], AF.Identity,
                                            bias=smalls[:, rope["cbias"] + oc:
                                                        rope["cbias"] + oc + 1]),
                               inc=ac, waits=[(pe, pv), (ve, rope["cbias_v"])])
                    setg(0, sl, sl + TS_, ac, vv)
                    last_ac = vv
                elif rope is None:
                    if PROJ_EVAC == "skip":
                        last_ve = last_ve
                    elif PROJ_EVAC == "dve_imm":
                        vv = em.op("vector", lambda e, dst=dst, sl=sl:
                                   e.tensor_scalar(dst, pA[:, sl:sl + TS_], 0.5,
                                                   None, OP.add),
                                   inc=ve, waits=[(pe, pv)])
                        setg(0, sl, sl + TS_, ve, vv)
                        last_ve = vv
                    elif PROJ_EVAC == "none":
                        vv = em.op("vector", lambda e, dst=dst, sl=sl:
                                   e.tensor_copy(dst, pA[:, sl:sl + TS_]),
                                   inc=ve, waits=[(pe, pv)])
                        setg(0, sl, sl + TS_, ve, vv)
                        last_ve = vv
                    elif PROJ_EVAC == "act":
                        vv = em.op("scalar", lambda e, dst=dst, bcol=bcol, sl=sl:
                                   e.activation(dst, pA[:, sl:sl + TS_], AF.Identity,
                                                bias=bcol),
                                   inc=ac, waits=[(pe, pv)])
                        setg(0, sl, sl + TS_, ac, vv)
                        last_ac = vv
                    else:
                        vv = em.op("vector", lambda e, dst=dst, bcol=bcol, sl=sl:
                                   e.tensor_scalar(dst, pA[:, sl:sl + TS_], bcol,
                                                   None, OP.add),
                                   inc=ve, waits=[(pe, pv)])
                        setg(0, sl, sl + TS_, ve, vv)
                        last_ve = vv
                else:
                    rbcol = colsS[:, rope["bias_c"] + oc:rope["bias_c"] + oc + 1]
                    av = em.op("scalar", lambda e, dst=dst, bcol=bcol, sl=sl:
                               e.activation(dst, pA[:, sl:sl + TS_], AF.Identity,
                                            bias=bcol), inc=ac, waits=[(pe, pv)])
                    setg(0, sl, sl + TS_, ac, av)
                    em.op("vector", lambda e, sl=sl, rbcol=rbcol: e.tensor_scalar(
                        pB[:, sl:sl + TS_], pB[:, sl:sl + TS_], rbcol, None, OP.add),
                        waits=[(pe, pv)])
                    if oc == 0:
                        sc1, sc2 = 1024, 1536
                        shmask = list(range(16, 32)) + list(range(0, 16))
                        gw = guards(0, sc1, 2048)
                        em.op("vector", lambda e, sl=sl: e.stream_shuffle(
                            pA[0:32, sc1:sc1 + TS_], pB[0:32, sl:sl + TS_], shmask),
                            waits=[(ld, ld_tab)] + gw)
                        em.op("vector", lambda e, t0=t0: e.tensor_mul(
                            pA[0:32, sc1:sc1 + TS_], pA[0:32, sc1:sc1 + TS_],
                            ropeT[32:64, t0:t0 + TS_]))
                        em.op("vector", lambda e, sl=sl, t0=t0: e.tensor_mul(
                            pA[0:32, sc2:sc2 + TS_], pB[0:32, sl:sl + TS_],
                            ropeT[0:32, t0:t0 + TS_]), waits=[(ac, av)])
                        em.op("vector", lambda e, dst=dst: e.tensor_add(
                            dst[0:32, :], dst[0:32, :], pA[0:32, sc1:sc1 + TS_]))
                        em.op("vector", lambda e, dst=dst: e.tensor_add(
                            dst[0:32, :], dst[0:32, :], pA[0:32, sc2:sc2 + TS_]))
                        em.op("vector", lambda e, dst=dst, sl=sl: e.tensor_add(
                            dst[32:64, :], dst[32:64, :], pB[32:64, sl:sl + TS_]))
                        vv = em.op("vector", lambda e, dst=dst, sl=sl: e.tensor_add(
                            dst[64:128, :], dst[64:128, :], pB[64:128, sl:sl + TS_]),
                            inc=ve)
                        setg(0, sc1, 2048, ve, vv)
                    else:
                        vv = em.op("vector", lambda e, dst=dst, sl=sl: e.tensor_add(
                            dst, dst, pB[:, sl:sl + TS_]), inc=ve,
                            waits=[(ac, av)])
                    setg(0, sl, sl + TS_, ve, vv)
                    setg(1, sl, sl + TS_, ve, vv)
                    last_ve = vv
        out = []
        if last_ve is not None:
            out.append((ve, last_ve))
        if last_ac is not None:
            out.append((ac, last_ac))
        return out

    q_done = k_done = v_done = [(ve, ln1_done)]
    core_v = x2_done = ln1_done
    if stage >= 2:
        em.wait("tensor", ve, ln1_done)
        if SUBSTAGE >= 1:
            ctq_done = proj(ct1, xnq, NCH, VW_DQ, 4, 8, C_BDQ, w_dq_ld)
        if SUBSTAGE >= 2:
            ctkv_done = proj(ct2, xnq, NCH, VW_DKV, 4, 8, C_BDKV, w_dkv_ld)
        pe_ct = pe.n
        if "reload" in DBG_DROP:
            w_uk_ld = w_dq_ld
            w_uq_ld = w_dkv_ld
        else:
            w_uk_ld = wload(VW_UK, "w_uk", 32, waits=[(pe, pe_ct)])
            w_uq_ld = wload(VW_UQ, "w_uq", 32, waits=[(pe, pe_ct)])
        for oc in range(8 if "cbias" not in DBG_DROP else 0):
            em.op("vector", lambda e, oc=oc: e.tensor_add(
                smalls[:, 21 + oc:22 + oc], colsS[:, C_BUK + oc:C_BUK + oc + 1],
                colsS[:, C_BKR + oc:C_BKR + oc + 1]), waits=[(ld, ld_misc)])
        cb_v = em.op("vector", lambda e: e.engine_nop(), inc=ve) if False else None
        for oc in range(8 if "cbias" not in DBG_DROP else 0):
            kq_v = em.op("vector", lambda e, oc=oc: e.tensor_add(
                smalls[:, 29 + oc:30 + oc], colsS[:, C_BUQ + oc:C_BUQ + oc + 1],
                colsS[:, C_BQR + oc:C_BQR + oc + 1]),
                inc=(ve if oc == 7 else None))
        if "cbias" in DBG_DROP:
            kq_v = em.op("vector", lambda e: e.memset(smalls[:, 21:37], 0.125),
                         inc=ve)
        cbias_v = kq_v

        if SUBSTAGE < 3:
            k_done = q_done = v_done = [(ve, ln1_done)]
        if SUBSTAGE >= 3:
            k_done = proj(khB, ct2, 4, VW_UK, 8, 4, C_BUK, w_uk_ld,
                          rope={"wcol": VW_KR, "n_ic": 8, "bias_c": C_BKR,
                                "w_ld": w_kr_ld, "cbias": 21, "cbias_v": cbias_v},
                          src2=xnq)
        if SUBSTAGE >= 4:
            pe_k = pe.n
            w_qr_ld = wload(VW_QR, "w_qr", 32, waits=[(pe, pe_k)])
            q_done = proj(xnq, ct1, 4, VW_UQ, 8, 4, C_BUQ, w_uq_ld,
                          rope={"wcol": VW_QR, "n_ic": 4, "bias_c": C_BQR,
                                "w_ld": w_qr_ld, "cbias": 29, "cbias_v": cbias_v},
                          src2=ct1)

        if SUBSTAGE < 9:
            v_done = 0
            if dbg:
                em.op("sync", lambda e: e.dma_start(dbg_t["d_q"], xnq[:, :]),
                      inc=st, k=16, waits=list(q_done))
                em.op("sync", lambda e: e.dma_start(dbg_t["d_k"], khB[:, :]),
                      inc=st, k=16, waits=list(k_done))
        _skip_v = SUBSTAGE < 9
        if not _skip_v:
            pe_qk = pe.n
            w_uv_ld = dma("gpsimd",
                          ct1w[:, 0:4096].rearrange("p (c f) -> p c f", c=4),
                          wuvD, waits=list(q_done))
            for half in range(2):
                h0 = half * TS_
                bw = guards(0, h0, h0 + TS_) + [(ld, ld_misc)]
                bv = mm(pA[:, h0:h0 + TS_], onesr[0:1, :],
                        vbrow[0:1, h0:h0 + TS_], True, True,
                        waits=bw, inc=pe)
                vv = em.op("vector", lambda e, h0=h0: e.tensor_copy(
                    spF[:, h0:h0 + TS_], pA[:, h0:h0 + TS_]), inc=ve,
                    waits=[(pe, bv)] + list(q_done))
                setg(0, h0, h0 + TS_, ve, vv)
            vbb_v = vv
            vt_ones = em.op("vector", lambda e: e.memset(vwB[:, 64:16640:65], 1.0),
                            inc=ve, waits=list(k_done) + list(q_done) + [(pe, pe_qk)])
            for tt in range(16):
                t0 = tt * 128
                for dvh in range(2):
                    c0 = dvh * TS_
                    bw = guards(1, c0, c0 + TS_)
                    for cc in range(4):
                        pv = mm(pB[:, c0:c0 + TS_],
                                ct2[:, cc * S + t0:cc * S + t0 + 128],
                                ct1w[:, cc * 1024 + dvh * TS_:cc * 1024 + (dvh + 1) * TS_],
                                start=(cc == 0), stop=(cc == 3),
                                waits=(bw + [(wst, w_uv_ld), (ve, vt_ones), (ve, vbb_v)])
                                if cc == 0 else [], inc=(pe if cc == 3 else None))

                    def vevac(e, tt=tt, dvh=dvh, c0=c0):
                        op = vwB[:, tt * 1040 + dvh * 520: tt * 1040 + dvh * 520 + 520]
                        op = op.rearrange("p (h c) -> p h c", c=65)[:, :, 0:64]
                        return e.tensor_add(
                            op, pB[:, c0:c0 + TS_].rearrange("p (h c) -> p h c", c=64),
                            spF[:, c0:c0 + TS_].rearrange("p (h c) -> p h c", c=64))
                    vv = em.op("vector", vevac, inc=ve, waits=[(pe, pv)])
                    setg(1, c0, c0 + TS_, ve, vv)
            v_done = vv

        if dbg:
            em.op("sync", lambda e: e.dma_start(dbg_t["d_q"], xnq[:, :]), inc=st,
                  k=16, waits=list(q_done))
            em.op("sync", lambda e: e.dma_start(dbg_t["d_k"], khB[:, :]), inc=st,
                  k=16, waits=list(k_done))
            em.op("sync", lambda e: e.dma_start(dbg_t["d_vt"], vwB[:, 0:16640]),
                  inc=st, k=16, waits=[(ve, v_done)])

    if stage >= 3:
        def norms(src, out_col, wv):
            pv = None
            rdpe = {}
            for c in range(NCH):
                sl = c % 2
                wts = list(wv)
                if c >= 2:
                    wts.append((pe, rdpe[sl]))
                sqdst = spF[:, 0:S] if sl == 0 else ct1F[:, 0:S]
                sq = em.op("vector", lambda e, c=c, d=sqdst: e.tensor_mul(
                    d, src[:, c * S:(c + 1) * S],
                    src[:, c * S:(c + 1) * S]), inc=ve, waits=wts)
                for ts in range(NTS):
                    t0 = ts * TS_
                    w0 = guards(0, t0, t0 + TS_) if c == 0 else []
                    pv = mm(pA[0:16, t0:t0 + TS_],
                            colsS[:, C_ONES16 + c * 16:C_ONES16 + (c + 1) * 16],
                            sqdst[:, t0:t0 + TS_],
                            start=(c == 0), stop=(c == NCH - 1),
                            waits=w0 + [(ve, sq)],
                            inc=(pe if ts == NTS - 1 else None))
                rdpe[sl] = pv
            rv = em.op("vector", lambda e, oc=out_col: e.reduce_max(
                smalls[0:16, oc:oc + 1], pA[0:16, 0:S], axis=AXX), inc=ve,
                waits=[(pe, pv)])
            setg(0, 0, S, ve, rv)
            return rv

        nq = norms(xnq, 0, list(q_done) + [(ve, v_done)])
        nk = norms(khB, 1, list(k_done) + [(ve, v_done)])
        em.op("vector", lambda e: e.tensor_add(smalls[0:16, 2:3], smalls[0:16, 0:1],
                                               smalls[0:16, 1:2]),
              waits=[(ve, nq), (ve, nk)])
        cnv = em.op("vector", lambda e: e.tensor_scalar(
            smalls[0:16, 2:3], smalls[0:16, 2:3],
            colsS[0:16, C_RSCALE:C_RSCALE + 1], -0.5, OP.mult, OP.mult), inc=ve)
        tw = guards(1, 0, TS_)
        tpv = em.op("tensor", lambda e: nc.tensor.transpose(
            pB[0:1, 0:16], smalls[0:16, 2:3], colsS[0:16, C_IDN16:C_IDN16 + 16]),
            inc=pe, waits=tw + [(ve, cnv)])
        crv = em.op("vector", lambda e: e.tensor_copy(smalls[0:1, 3:19],
                                                      pB[0:1, 0:16]),
                    inc=ve, waits=[(pe, tpv)])
        setg(1, 0, TS_, ve, crv)
        cbv = mm(pB[:, 0:16], onesr[0:1, :], smalls[0:1, 3:19],
                 True, True, waits=guards(1, 0, TS_), inc=pe)
        cbev = em.op("vector", lambda e: e.tensor_copy(smalls[:, 4:20],
                                                       pB[:, 0:16]),
                     inc=ve, waits=[(pe, cbv)])
        setg(1, 0, TS_, ve, cbev)
        if dbg:
            em.op("sync", lambda e: e.dma_start(dbg_t["d_cb"], smalls[:, 4:20]),
                  inc=st, k=16, waits=[(ve, cbev)])

        ohalf = (ct1, ct2)
        pt_guard = {0: None, 1: None}
        rb_prev = None
        for qh in range(2):
            q0 = qh * 1024
            for h in range(NH):
                c = h // 2
                r0 = 64 * (h % 2)
                av_of = {}
                for kc in range(16):
                    stc = (kc % 2) * 1024
                    bw = guards(0, stc, stc + 1024)
                    for half in range(2):
                        pv = mm(pA[:, stc + half * TS_:stc + (half + 1) * TS_],
                                khB[r0:r0 + 64, c * S + kc * 128:c * S + kc * 128 + 128],
                                xnq[r0:r0 + 64, c * S + q0 + half * TS_:c * S + q0 + half * TS_ + TS_],
                                start=True, stop=True,
                                waits=(bw + [(ve, cbev)]) if half == 0 else [],
                                inc=(pe if half == 1 else None))
                    expw = [(pe, pv)]
                    if pt_guard[kc % 2] is not None:
                        expw.append(pt_guard[kc % 2])
                    ev = em.op("scalar", lambda e, stc=stc, h=h, kc=kc: e.activation(
                        spB[:, PT[kc % 2]:PT[kc % 2] + 1024], pA[:, stc:stc + 1024],
                        AF.Exp, bias=smalls[:, 4 + h:5 + h],
                        scale=colsS[:, C_RSCALE:C_RSCALE + 1]),
                        inc=ac, waits=expw)
                    setg(0, stc, stc + 1024, ac, ev)
                    ow = guards(1, 0, 1024) if kc == 0 else []
                    for half in range(2):
                        av = mm(pB[0:65, half * TS_:(half + 1) * TS_],
                                vwB[:, kc * 1040 + h * 65:kc * 1040 + h * 65 + 65],
                                spB[:, PT[kc % 2] + half * TS_:PT[kc % 2] + (half + 1) * TS_],
                                start=(kc == 0), stop=(kc == 15),
                                waits=(ow + [(ac, ev)]) if half == 0 else [],
                                inc=(pe if half == 1 else None))
                    av_of[kc] = av
                    pt_guard[kc % 2] = (pe, av)
                O = ohalf[qh]
                ov = em.op("vector", lambda e, O=O, c=c, r0=r0: e.tensor_copy(
                    O[r0:r0 + 64, c * 1024:(c + 1) * 1024], pB[0:64, 0:1024]),
                    inc=ve, waits=[(pe, av_of[15])])
                dv = em.op("scalar", lambda e: e.activation(
                    sprow[0:1, 0:1024], pB[64:65, 0:1024], AF.Ln), inc=ac,
                    waits=[(pe, av_of[15])])
                rvw = [] if rb_prev is None else [(pe, rb_prev)]
                rv = em.op("scalar", lambda e: e.activation(
                    rinvrow[0:1, 0:1024], sprow[0:1, 0:1024], AF.Exp, scale=-1.0),
                    inc=ac, waits=rvw)
                setg(1, 0, 1024, ve, ov)
                setg(1, 0, 1024, ac, dv)
                nw = guards(1, 1024, 2048)
                for half in range(2):
                    rb_prev = mm(pB[0:64, 1024 + half * TS_:1024 + (half + 1) * TS_],
                                 onesr[0:1, 0:64],
                                 rinvrow[0:1, half * TS_:(half + 1) * TS_],
                                 True, True,
                                 waits=(nw + [(ac, rv)]) if half == 0 else [], inc=pe)
                nv = em.op("vector", lambda e, O=O, c=c, r0=r0: e.tensor_mul(
                    O[r0:r0 + 64, c * 1024:(c + 1) * 1024],
                    O[r0:r0 + 64, c * 1024:(c + 1) * 1024], pB[0:64, 1024:2048]),
                    inc=ve, waits=[(pe, rb_prev), (ve, ov)])
                setg(1, 1024, 2048, ve, nv)
        core_v = ve.n
        if dbg:
            em.op("sync", lambda e: e.dma_start(dbg_t["d_oh0"], ct1[:, :]), inc=st,
                  k=16, waits=[(ve, core_v)])
            em.op("sync", lambda e: e.dma_start(dbg_t["d_oh1"], ct2[:, :]), inc=st,
                  k=16, waits=[(ve, core_v)])

        pe_core = pe.n
        wo_ld = dma("gpsimd",
                    khB[:, 0:8192].rearrange("p (a f) -> p a f", f=128),
                    wD["w_o"], waits=[(pe, pe_core)])
        for f in range(NCH):
            for qh in range(2):
                O = ohalf[qh]
                for half in range(2):
                    t0 = qh * 1024 + half * TS_
                    pc = (qh * 2 + half) * TS_
                    bw = guards(0, pc, pc + TS_)
                    for dm in range(NCH):
                        pv = mm(pA[:, pc:pc + TS_],
                                khB[:, (f * NCH + dm) * 128:(f * NCH + dm) * 128 + 128],
                                O[:, dm * 1024 + half * TS_:dm * 1024 + (half + 1) * TS_],
                                start=(dm == 0), stop=(dm == NCH - 1),
                                waits=(bw + [(wst, wo_ld), (ve, core_v)])
                                if dm == 0 else [],
                                inc=(pe if dm == NCH - 1 else None))
                    vv = em.op("vector", lambda e, f=f, t0=t0, pc=pc:
                               e.scalar_tensor_tensor(
                        xres[:, f * S + t0:f * S + t0 + TS_], pA[:, pc:pc + TS_],
                        colsS[:, C_BO + f:C_BO + f + 1],
                        xres[:, f * S + t0:f * S + t0 + TS_], OP.add, OP.add),
                        inc=ve, waits=[(pe, pv)])
                    setg(0, pc, pc + TS_, ve, vv)
        x2_done = ve.n
        if dbg:
            em.op("sync", lambda e: e.dma_start(
                dbg_t["d_x2"].rearrange("(c p) t -> p c t", p=128),
                xres[:, :].rearrange("p (c t) -> p c t", c=NCH)),
                inc=st, k=16, waits=[(ve, x2_done)])

    ln2_done = x2_done
    mrow_v = None
    if stage >= 4:
        pe_wo = pe.n
        ln2_done = layernorm(xnq, (khF[:, 0:S], khF[:, S:2 * S]),
                             wait_x_pe=pe_wo, wait_x_ve=x2_done)
        hv = None
        for ts in range(NTS):
            t0 = ts * TS_
            w0 = (guards(0, 0, S) + guards(1, 0, S) + [(ve, x2_done)]) if ts == 0 else []
            hv = mm(pA[0:1, t0:t0 + TS_], colsS[0:2, C_ONE:C_ONE + 1],
                    xres[0:2, t0:t0 + TS_], True, True, waits=w0,
                    inc=(pe if ts == NTS - 1 else None))
        em.op("vector", lambda e: e.tensor_scalar(vrow, murow, 2.0, None,
                                                  OP.mult),
              waits=[(pe, hv), (ve, ln2_done)])
        em.op("vector", lambda e: e.tensor_sub(pA[0:1, 0:S], pA[0:1, 0:S],
                                               vrow))
        em.op("vector", lambda e: e.tensor_mul(vrow, pA[0:1, 0:S], rstdrow))
        em.op("vector", lambda e: e.tensor_copy(hi1[0:1, :], vrow))
        em.op("vector", lambda e: e.tensor_copy(trow, hi1[0:1, :]))
        em.op("vector", lambda e: e.tensor_tensor(pA[0:1, 0:S], trow, vrow, OP.is_gt))
        em.op("vector", lambda e: e.tensor_tensor(pB[0:1, 0:S], trow, vrow, OP.is_lt))
        em.op("vector", lambda e: e.tensor_scalar(sprow_sp, vrow, 0.0, None, OP.is_ge))
        em.op("vector", lambda e: e.tensor_mul(pA[0:1, 0:S], pA[0:1, 0:S], sprow_sp))
        em.op("vector", lambda e: e.tensor_mul(tmprow, pB[0:1, 0:S], sprow_sp))
        em.op("vector", lambda e: e.tensor_sub(pB[0:1, 0:S], pB[0:1, 0:S], tmprow))
        em.op("vector", lambda e: e.tensor_sub(trow, trow, pA[0:1, 0:S]))
        em.op("vector", lambda e: e.tensor_add(trow, trow, pB[0:1, 0:S]))
        em.op("vector", lambda e: e.tensor_copy(hi1[0:1, :], trow))
        em.op("vector", lambda e: e.tensor_scalar(hi1[0:1, :], hi1[0:1, :], 64,
                                                  None, OP.add))
        em.op("vector", lambda e: e.tensor_scalar(hi1[0:1, :], hi1[0:1, :], 7,
                                                  None, OP.bitwise_and))
        mrow_v = em.op("vector", lambda e: e.tensor_copy(mrow, hi1[0:1, :]), inc=ve)
        setg(0, 0, S, ve, mrow_v)
        setg(1, 0, S, ve, mrow_v)

        if dbg:
            em.op("sync", lambda e: e.dma_start(dbg_t["d_mrow"], mrow), inc=st,
                  k=16, waits=[(ve, mrow_v)])

    if stage >= 5:
        W1O = (0, 1024)
        W2O = 2048

        w1_ld = {}
        w2_ld = {}
        w1_free = {}

        w1_ld[(0, 0)] = dma("gpsimd",
                            vwB[:, W1O[0]:W1O[0] + 1024].rearrange("p (b f) -> p b f", f=128),
                            w1D[0, 0],
                            waits=[(ve, x2_done)])
        w2_ld[(0, 0)] = dma("gpsimd",
                            vwB[:, W2O:W2O + 8192].rearrange("p (b f) -> p b f", f=128),
                            w2D[0, 0],
                            waits=[(ve, x2_done)])

        for e_ in range(min(NE, FFN_NE)):
            em.op("vector", lambda e, e_=e_: e.tensor_scalar(
                eqa, mrow, float(e_), None, OP.is_equal), waits=[(ve, mrow_v)])
            em.op("vector", lambda e, e_=e_: e.tensor_scalar(
                eqb, mrow, float((e_ - 1) % NE), None, OP.is_equal))
            wrv = em.op("vector", lambda e: e.tensor_add(wrowB[0:1, :], eqa, eqb),
                        inc=ve)
            mv = None
            for ts in range(NTS):
                t0 = ts * TS_
                bw = guards(1, t0, t0 + TS_)
                pv = mm(pB[:, t0:t0 + TS_], halfB[0:1, :],
                        wrowB[0:1, t0:t0 + TS_], True, True,
                        waits=bw + [(ve, wrv)], inc=pe)
                mv = em.op("vector", lambda e, t0=t0: e.tensor_copy(
                    spB[:, MB + t0:MB + t0 + TS_], pB[:, t0:t0 + TS_]), inc=ve,
                    waits=[(pe, pv)])
                setg(1, t0, t0 + TS_, ve, mv)
            mask_done = mv

            for g in range(4):
                h_last = None
                for ml in range(G):
                    m = g * G + ml
                    pt_id = ml % 2
                    pten = PTEN[pt_id]
                    slot = W1O[m % 2]
                    nm = m + 1
                    ne_, nm_ = (e_, nm) if nm < 32 else (e_ + 1, 0)
                    if ne_ < NE and (ne_, nm_) not in w1_ld:
                        w1_ld[(ne_, nm_)] = dma(
                            "gpsimd",
                            vwB[:, W1O[nm_ % 2]:W1O[nm_ % 2] + 1024].rearrange(
                                "p (b f) -> p b f", f=128),
                            w1D[ne_, nm_],
                            waits=[w1_free.get((ne_, nm_))])
                    pv = None
                    for ts in range(NTS):
                        t0 = ts * TS_
                        bw = guards(pt_id, t0, t0 + TS_)
                        for cc in range(NCH):
                            pv = mm(pten[:, t0:t0 + TS_],
                                    vwB[:, slot + cc * 128:slot + cc * 128 + 128],
                                    xnq[:, cc * S + t0:cc * S + t0 + TS_],
                                    start=(cc == 0), stop=(cc == NCH - 1),
                                    waits=(bw + [(wst, w1_ld[(e_, m)]),
                                                 (ve, ln2_done)]) if cc == 0 else [],
                                    inc=(pe if (cc == NCH - 1 and ts == NTS - 1)
                                         else None))
                    fm = m + 2
                    fe_, fm_ = (e_, fm) if fm < 32 else (e_ + 1, fm - 32)
                    w1_free[(fe_, fm_)] = (pe, pv)
                    for hts in range(NTS):
                        h0 = hts * TS_
                        hv = em.op("scalar", lambda e, m=m, pten=pten, e_=e_, h0=h0:
                                   e.activation(
                            khB[:, (m % G) * S + h0:(m % G) * S + h0 + TS_],
                            pten[:, h0:h0 + TS_], AF.Relu,
                            bias=colsS[:, C_B1 + e_ * 32 + m:
                                       C_B1 + e_ * 32 + m + 1]),
                            inc=ac, waits=([(pe, pv)] if hts == 0 else []))
                    setg(pt_id, 0, S, ac, hv)
                    h_last = em.op("vector", lambda e, m=m: e.tensor_mul(
                        khB[:, (m % G) * S:(m % G) * S + S],
                        khB[:, (m % G) * S:(m % G) * S + S],
                        spB[:, MB:MB + S]), inc=ve,
                        waits=[(ac, hv), (ve, mask_done)])
                yt = 0
                ylast_pe = None
                for f in range(NCH):
                    for ts in range(NTS):
                        t0 = ts * TS_
                        slot4 = yt % 4
                        yt += 1
                        pt_id = slot4 // 2
                        pten = PTEN[pt_id]
                        pc = (slot4 % 2) * TS_
                        bw = guards(pt_id, pc, pc + TS_)
                        for ml in range(G):
                            ylast_pe = mm(
                                pten[:, pc:pc + TS_],
                                vwB[:, W2O + (f * G + ml) * 128:W2O + (f * G + ml) * 128 + 128],
                                khB[:, (ml) * S + t0:(ml) * S + t0 + TS_],
                                start=(ml == 0),
                                stop=(ml == G - 1),
                                waits=(bw + [(wst, w2_ld[(e_, g)]),
                                             (ve, h_last)]) if ml == 0 else [],
                                inc=(pe if ml == G - 1 else None))
                        if g == 0:
                            em.op("vector", lambda e, f=f, t0=t0, e_=e_: e.tensor_scalar(
                                spF[:, 1024:1024 + TS_], spB[:, MB + t0:MB + t0 + TS_],
                                colsS[:, C_B2 + e_ * 8 + f:C_B2 + e_ * 8 + f + 1],
                                None, OP.mult), waits=[(pe, ylast_pe)])
                            em.op("vector", lambda e, f=f, t0=t0: e.tensor_add(
                                xres[:, f * S + t0:f * S + t0 + TS_],
                                spF[:, 1024:1024 + TS_],
                                xres[:, f * S + t0:f * S + t0 + TS_]))
                        yv = em.op("vector", lambda e, f=f, t0=t0, pten=pten, pc=pc:
                                   e.tensor_add(
                            xres[:, f * S + t0:f * S + t0 + TS_],
                            pten[:, pc:pc + TS_],
                            xres[:, f * S + t0:f * S + t0 + TS_]),
                            inc=ve, waits=[(pe, ylast_pe)])
                        setg(pt_id, pc, pc + TS_, ve, yv)
                if dbg and e_ == 0 and g == 0:
                    em.op("sync", lambda e: e.dma_start(dbg_t["d_h"], khB[:, :]),
                          inc=st, k=16, waits=[(ve, h_last)])
                    em.op("sync", lambda e: e.dma_start(dbg_t["d_mask"], spB[:, 0:S]),
                          inc=st, k=16, waits=[(ve, mask_done)])
                    em.op("sync", lambda e: e.dma_start(
                        dbg_t["d_w2"], vwB[:, W2O:W2O + 8192]),
                        inc=st, k=16, waits=[(wst, w2_ld[(0, 0)])])
                ng, ne2 = (g + 1, e_) if g < 3 else (0, e_ + 1)
                if ne2 < NE and (ne2, ng) not in w2_ld:
                    w2_ld[(ne2, ng)] = dma(
                        "gpsimd",
                        vwB[:, W2O:W2O + 8192].rearrange("p (b f) -> p b f", f=128),
                        w2D[ne2, ng],
                        waits=[(pe, ylast_pe)])

    final_v = ve.n
    out3 = out_x.rearrange("(c p) t -> c p t", p=128)
    for c in range(NCH):
        w = [(ve, final_v)] if c == 0 else []
        if c >= 2:
            w.append((st, st.n - 16))
        em.op("sync", lambda e, c=c, w=w: e.dma_start(
            out3[c], xres[:, c * S:(c + 1) * S]), inc=st, k=16, waits=w)
    em.wait("sync", ld, ld.n)
    em.wait("sync", wst, wst.n)
    em.op("sync", lambda e, v=st.n: e.wait_ge(st.h, v))

    with nc.Block() as block:
        for name, deco in (("sync", block.sync), ("gpsimd", block.gpsimd),
                           ("vector", block.vector), ("scalar", block.scalar),
                           ("tensor", block.tensor)):
            ops = em.plan[name]

            def body(eng, ops=ops):
                for f in ops:
                    f(eng)
            deco(body)
    ctx.close()
    return nc, dbg_t



def kernel(**inputs):
    shared = prep_host(inputs)
    x = np.asarray(inputs["L0_attn_x"], np.float32)
    nc, _ = build_program(stage=BUILD_STAGE, dbg=DEBUG_OUTS)
    in_maps = []
    for c in range(8):
        m = {"x_in": np.ascontiguousarray(x[c].T)}
        m.update(shared)
        in_maps.append(m)
    res = run_bass_kernel_spmd(nc, in_maps, core_ids=list(range(8)))
    out = np.stack([np.ascontiguousarray(res.results[c]["out_x"].T)
                    for c in range(8)])
    return out
